# version 37
# baseline (speedup 1.0000x reference)
import contextlib

import numpy as np
import ml_dtypes
import concourse.bass as bass
import concourse.mybir as mybir
from concourse.bass_utils import run_bass_kernel_spmd

F32 = mybir.dt.float32
BF16 = mybir.dt.bfloat16
I32 = mybir.dt.int32
AF = mybir.ActivationFunctionType
ALU = mybir.AluOpType
AX = mybir.AxisListType

T = 2048
D = 1024
H = 3072
NT = T // 128
NDC = D // 128
NHC = H // 128
HBLK = 4
NBLK = NHC // HBLK
S127 = 1.0 / 127.0
S127SQ = (1.0 / 127.0) ** 2
EPS = 1e-5


def build_nc():
    nc = bass.Bass("TRN2", target_bir_lowering=False)

    x_hbm = nc.dram_tensor("x", [T, D], F32, kind="ExternalInput").ap()
    wg_hbm = nc.dram_tensor("wg", [H, D], I32, kind="ExternalInput").ap()
    wu_hbm = nc.dram_tensor("wu", [H, D], I32, kind="ExternalInput").ap()
    wd_hbm = nc.dram_tensor("wd", [D, H], I32, kind="ExternalInput").ap()
    id_hbm = nc.dram_tensor("ident", [128, 128], BF16, kind="ExternalInput").ap()
    o_hbm = nc.dram_tensor("o", [T, D], F32, kind="ExternalOutput").ap()

    wg_bf = nc.dram_tensor("wg_bf", [H, D], BF16, kind="Internal").ap()
    wu_bf = nc.dram_tensor("wu_bf", [H, D], BF16, kind="Internal").ap()
    wd_bf = nc.dram_tensor("wd_bf", [D, H], BF16, kind="Internal").ap()
    p_dram = nc.dram_tensor("p_dram", [T, D], BF16, kind="Internal").ap()

    x_r = x_hbm.rearrange("(t p) d -> p t d", p=128)
    p_r = p_dram.rearrange("(t p) d -> p t d", p=128)
    o_r = o_hbm.rearrange("(t p) d -> p t d", p=128)

    with contextlib.ExitStack() as ctx:
        def sb(name, shape, dtype):
            return ctx.enter_context(nc.sbuf_tensor(name, shape, dtype))

        x_sb = sb("x_sb", [128, NT * D], F32)
        xnT = sb("xnT", [128, NDC * T], BF16)
        arena = sb("arena", [128, 40960], BF16)
        scr32 = sb("scr32", [128, 4096], F32)
        ptile = sb("ptile", [128, 2 * D], BF16)
        xnb = sb("xnb", [128, 4 * D], BF16)
        sb_id = sb("sb_id", [128, 128], BF16)
        st_eps = sb("st_eps", [128, 1], F32)
        st_bn = sb("st_bn", [128, 24], F32)
        st_mv2 = sb("st_mv2", [128, 2 * NT], F32)
        st_lnv = sb("st_lnv", [128, NT], F32)
        st_rstd = sb("st_rstd", [128, NT], F32)
        st_s = sb("st_s", [128, NT], F32)
        st_rs = sb("st_rs", [128, 2], F32)
        rs_bf = sb("rs_bf", [128, NT], BF16)
        recd = sb("recd", [128, 2], F32)
        s_lx = ctx.enter_context(nc.semaphore("lx"))
        s_ly = ctx.enter_context(nc.semaphore("ly"))
        s_lz = ctx.enter_context(nc.semaphore("lz"))
        s_lv = ctx.enter_context(nc.semaphore("lv"))
        s_l4 = ctx.enter_context(nc.semaphore("l4"))
        s_l5 = ctx.enter_context(nc.semaphore("l5"))
        s_l6 = ctx.enter_context(nc.semaphore("l6"))
        s_l7 = ctx.enter_context(nc.semaphore("l7"))
        s_li = ctx.enter_context(nc.semaphore("li"))
        s_pa0 = ctx.enter_context(nc.semaphore("pa0"))
        s_pa1 = ctx.enter_context(nc.semaphore("pa1"))
        s_t0g = ctx.enter_context(nc.semaphore("t0g"))
        s_tg = ctx.enter_context(nc.semaphore("tg"))
        s_g0 = ctx.enter_context(nc.semaphore("g0"))
        s_g1 = ctx.enter_context(nc.semaphore("g1"))
        s_d0 = ctx.enter_context(nc.semaphore("d0"))
        s_d1 = ctx.enter_context(nc.semaphore("d1"))
        s_ca = ctx.enter_context(nc.semaphore("ca"))
        s_st = ctx.enter_context(nc.semaphore("st"))
        s_pe = ctx.enter_context(nc.semaphore("pe"))
        s_ac = ctx.enter_context(nc.semaphore("ac"))
        s_dv = ctx.enter_context(nc.semaphore("dv"))
        block = ctx.enter_context(nc.Block())
        sems = {"lx": s_lx, "ly": s_ly, "lz": s_lz, "lv": s_lv,
                "l4": s_l4, "l5": s_l5, "l6": s_l6, "l7": s_l7,
                "li": s_li, "pa0": s_pa0, "pa1": s_pa1,
                "t0g": s_t0g, "tg": s_tg, "g0": s_g0, "g1": s_g1,
                "d0": s_d0, "d1": s_d1, "ca": s_ca, "st": s_st,
                "pe": s_pe, "ac": s_ac, "dv": s_dv}
        cnt = {k: 0 for k in sems}

        def mark(instr, sem, k=1):
            instr.then_inc(sems[sem], k)
            cnt[sem] += k
            return (sem, cnt[sem])

        def wt(eng, ev):
            if ev is not None:
                eng.wait_ge(sems[ev[0]], ev[1])

        def now(sem):
            return (sem, cnt[sem])

        PT0, SC0, XT0 = 0, 16384, 24576

        def pT_sl(dc, i0, n):
            return arena[:, PT0 + dc * T + i0: PT0 + dc * T + i0 + n]

        def sc_sl(jc, i0, n):
            return arena[:, SC0 + jc * 512 + i0: SC0 + jc * 512 + i0 + n]

        def xntok_sl(t, d0, n):
            return arena[:, XT0 + t * D + d0: XT0 + t * D + d0 + n]

        def wgT_sl(b2, c0, n):
            return arena[:, b2 * 4096 + c0: b2 * 4096 + c0 + n]

        def wuT_sl(b2, c0, n):
            return arena[:, 8192 + b2 * 4096 + c0: 8192 + b2 * 4096 + c0 + n]

        def wdT_sl(b2, c0, n):
            return arena[:, 16384 + b2 * 4096 + c0: 16384 + b2 * 4096 + c0 + n]

        def hact_sl(c0, n):
            return arena[:, XT0 + c0: XT0 + c0 + n]

        def xs(t):
            return x_sb[:, t * D:(t + 1) * D]

        def xnT_sl(dc, t0, n):
            return xnT[:, dc * T + t0: dc * T + t0 + n]

        E = {}

        @block.gpsimd
        def _(g):
            mark(g.dma_start(out=sb_id[:, :], in_=id_hbm[:, :]), "li", 16)
            for qi, sm in enumerate(("lx", "ly", "lz", "lv",
                                     "l4", "l5", "l6", "l7")):
                mark(g.dma_start(
                    out=x_sb[:, qi * 2 * D:(qi + 1) * 2 * D].rearrange(
                        "p (t d) -> p t d", t=2),
                    in_=x_r[:, qi * 2:(qi + 1) * 2, :]), sm, 16)
            for r in range(0, H, 768):
                mark(g.dma_start(out=wg_bf[r:r + 768, :],
                                 in_=wg_hbm[r:r + 768, :]), "ca", 16)
                mark(g.dma_start(out=wu_bf[r:r + 768, :],
                                 in_=wu_hbm[r:r + 768, :]), "ca", 16)
            for r in range(0, D, 256):
                mark(g.dma_start(out=wd_bf[r:r + 256, :],
                                 in_=wd_hbm[r:r + 256, :]), "ca", 16)
        E["ident"] = ("li", 16)
        for qi, sm in enumerate(("lx", "ly", "lz", "lv",
                                 "l4", "l5", "l6", "l7")):
            E[("xq8", qi)] = (sm, 16)

        @block.vector
        def _(v):
            mark(v.memset(st_eps[:, :], EPS), "dv")
        E["consts"] = now("dv")

        ps47 = ctx.enter_context(
            nc.psum_tensor("ps47", [128, 2048], F32, side="right"))

        last_pair_evac = {}

        def emit_m1_pair_te(sg, p):
            jcs = (2 * p, 2 * p + 1)
            reg = ps47[:, (p % 2) * 1024: (p % 2) * 1024 + 1024]

            @block.tensor
            def _(te, sg=sg, p=p, jcs=jcs, reg=reg):
                if p < 2:
                    if sg == 0:
                        wt(te, ("t0g", 128))
                    else:
                        wt(te, ("tg", 384))
                wt(te, E[("xnTe", 2 * p + 1)])
                wt(te, last_pair_evac.get(p % 2))
                if sg > 0 and p == 1:
                    wt(te, E[("recq", (sg - 1) * 4 + 3)])
                i0 = sg * 512
                for k, jc in enumerate(jcs):
                    for dc in range(NDC):
                        ins = te.matmul(
                            reg[:, k * 512:(k + 1) * 512],
                            xnT_sl(dc, jc * 128, 128),
                            pT_sl(dc, i0, 512),
                            start=(dc == 0), stop=(dc == NDC - 1))
                E[("scp", sg, p)] = mark(ins, "pe")

        def emit_m1_evac(sg, plist, first=False):
            @block.scalar
            def _(a, sg=sg, plist=plist, first=first):
                if first and sg > 0:
                    wt(a, E[("M2last", sg - 1)])
                for p in plist:
                    wt(a, E[("scp", sg, p)])
                    reg = ps47[:, (p % 2) * 1024: (p % 2) * 1024 + 1024]
                    for k, jc in enumerate((2 * p, 2 * p + 1)):
                        ins = a.activation(
                            sc_sl(jc, 0, 512), reg[:, k * 512:(k + 1) * 512],
                            AF.Exp)
                    E[("scpe", sg, p)] = mark(ins, "ac")
                    last_pair_evac[p % 2] = E[("scpe", sg, p)]

        def emit_m1_pair(sg, p):
            emit_m1_pair_te(sg, p)
            emit_m1_evac(sg, [p], first=(p == 0))

        with nc.psum_tensor("ps_bf", [128, 4096], BF16) as ps_bf:
            def emit_pass1(t):
                @block.vector
                def _(v, t=t):
                    qb = (t % 2) * 12
                    wt(v, E[("xq8", t // 2)])
                    wt(v, E.get(("mv", t - 2)))
                    mark(v.bn_stats(st_bn[:, qb:qb + 6], xs(t)[:, 0:512]),
                         "dv")
                    e1 = mark(v.bn_stats(st_bn[:, qb + 6:qb + 12],
                                         xs(t)[:, 512:1024]), "dv")
                    wt(v, e1)
                    E[("mv", t)] = mark(
                        v.bn_aggr(st_mv2[:, 2 * t: 2 * t + 2],
                                  st_bn[:, qb:qb + 12]), "dv")

            var_ap3 = st_mv2[:, :].rearrange("p (t two) -> p t two", two=2)
            def emit_pass2(pair):
                lo = pair * 2
                @block.scalar
                def _(a, lo=lo):
                    wt(a, E[("mv", lo + 1)])
                    e2 = mark(a.activation(
                        st_lnv[:, lo:lo + 2],
                        var_ap3[:, lo:lo + 2, 1:2], AF.Ln,
                        bias=st_eps[:, :]), "ac")
                    wt(a, e2)
                    E[("rstd", lo)] = mark(a.activation(
                        st_rstd[:, lo:lo + 2], st_lnv[:, lo:lo + 2],
                        AF.Exp, scale=-0.5), "ac")

            for t in range(8):
                emit_pass1(t)
            for pr in range(4):
                emit_pass2(pr)

            def emit_xnTe(eng, t):
                qp = t % 2
                dst = xnT[:, :].rearrange("p (dc t) -> p dc t", dc=NDC)[
                    :, :, t * 128:(t + 1) * 128]
                srcx = ps_bf[:, qp * 1024:(qp + 1) * 1024].rearrange(
                    "p (dc t) -> p dc t", dc=NDC)
                if hasattr(eng, "activation"):
                    E[("xnTe", t)] = mark(
                        eng.activation(dst, srcx, AF.Copy), "ac")
                else:
                    E[("xnTe", t)] = mark(
                        eng.tensor_copy(out=dst, in_=srcx), "dv")

            def emit_dve_pre(t):
                @block.vector
                def _(v, t=t):
                    q = t % 2
                    tq = t % 4
                    if t % 2 == 0:
                        wt(v, E[("rstd", t)])
                    xn32 = scr32[:, tq * D:(tq + 1) * D]
                    e3 = mark(v.tensor_scalar(
                        out=xn32, in0=xs(t),
                        scalar1=st_mv2[:, 2 * t: 2 * t + 1],
                        scalar2=st_rstd[:, t:t + 1],
                        op0=ALU.subtract, op1=ALU.mult), "dv")
                    wt(v, e3)
                    wt(v, E.get(("xnTr", t - 4)))
                    E[("xncast", t)] = mark(
                        v.tensor_copy(out=xnb[:, tq * D:(tq + 1) * D],
                                      in_=xn32), "dv")

            def emit_dve_post(t, with_pte=True):
                @block.vector
                def _(v, t=t):
                    q = t % 2
                    tq = t % 4
                    wt(v, E[("exp", t)])
                    e5 = mark(v.reciprocal(st_rs[:, q:q + 1],
                                           st_s[:, t:t + 1]), "dv")
                    wt(v, e5)
                    if t >= 2:
                        wt(v, ("pa%d" % (t % 2), 16 * ((t - 2) // 2 + 1)))
                    e32 = scr32[:, tq * D:(tq + 1) * D]
                    E[("pcast", t)] = mark(v.tensor_scalar(
                        out=ptile[:, q * D:(q + 1) * D], in0=e32,
                        scalar1=st_rs[:, q:q + 1], scalar2=None,
                        op0=ALU.mult), "dv")
                    mark(v.tensor_copy(out=rs_bf[:, t:t + 1],
                                       in_=st_rs[:, q:q + 1]), "dv")
                    E[("xnw", t)] = mark(v.tensor_scalar(
                        out=xntok_sl(t, 0, D),
                        in0=xnb[:, tq * D:(tq + 1) * D],
                        scalar1=st_rs[:, q:q + 1], scalar2=None,
                        op0=ALU.mult), "dv")

            def emit_p_store(t):
                @block.sync
                def _(sy, t=t):
                    wt(sy, E[("pcast", t)])
                    q = t % 2
                    mark(sy.dma_start(out=p_r[:, t, :],
                                      in_=ptile[:, q * D:(q + 1) * D]),
                         "pa%d" % (t % 2), 16)
                if t % 4 == 3:
                    g = t // 4
                    @block.sync
                    def _(sy, g=g):
                        thr = 32 * (g + 1)
                        sy.wait_ge(sems["pa0"], thr)
                        sy.wait_ge(sems["pa1"], thr)
                        sm = "t0g" if g == 0 else "tg"
                        for dc in range(NDC):
                            mark(sy.dma_start(
                                out=pT_sl(dc, g * 512, 512),
                                in_=p_dram[g * 512:(g + 1) * 512,
                                           dc * 128:(dc + 1) * 128],
                                transpose=True), sm, 16)

            def emit_pe_transp_xn(t):
                @block.tensor
                def _(te, t=t):
                    q = t % 2
                    tq = t % 4
                    wt(te, E["ident"])
                    wt(te, E[("xncast", t)])
                    wt(te, E.get(("xnTe", t - 2)))
                    for dc in range(NDC):
                        ins = te.transpose(
                            ps_bf[:, q * 1024 + dc * 128:
                                  q * 1024 + dc * 128 + 128],
                            xnb[:, tq * D + dc * 128: tq * D + dc * 128 + 128],
                            sb_id[:, :])
                    E[("xnTr", t)] = mark(ins, "pe")

            def emit_pe_transp(t):
                emit_pe_transp_xn(t)
                emit_p_store(t)

            emit_dve_pre(0)
            emit_dve_pre(1)
            for k in range(NT // 2):
                t0, t1 = 2 * k, 2 * k + 1

                @block.scalar
                def _(a, k=k, t0=t0, t1=t1):
                    wt(a, E[("xncast", t0)])
                    wt(a, E[("xncast", t1)])
                    wt(a, E.get(("pcast", t0 - 4)))
                    wt(a, E.get(("pcast", t1 - 4)))
                    if k >= 1:
                        wt(a, E[("xnTr", t0 - 2)])
                        wt(a, E[("xnTr", t0 - 1)])
                    if k < 6:
                        for t in (t0, t1):
                            tq = t % 4
                            xn32 = scr32[:, tq * D:(tq + 1) * D]
                            E[("exp", t)] = mark(
                                a.activation(xn32, xn32, AF.Exp,
                                             accum_out=st_s[:, t:t + 1]),
                                "ac")
                    if k >= 1:
                        emit_xnTe(a, t0 - 2)
                        emit_xnTe(a, t0 - 1)

                if k < NT // 2 - 1:
                    if k < 4:
                        emit_pass1(8 + 2 * k)
                        emit_pass1(9 + 2 * k)
                        if k == 1:
                            emit_pass2(4)
                            emit_pass2(5)
                        if k == 3:
                            emit_pass2(6)
                            emit_pass2(7)
                    emit_dve_pre(t0 + 2)
                    emit_dve_pre(t1 + 2)
                if k < 6:
                    emit_dve_post(t0)
                    emit_pe_transp(t0)
                    emit_dve_post(t1)
                    emit_pe_transp(t1)
                else:
                    emit_pe_transp_xn(t0)
                    emit_pe_transp_xn(t1)

            @block.scalar
            def _(a):
                wt(a, E[("xnTr", NT - 2)])
                wt(a, E[("xnTr", NT - 1)])
                emit_xnTe(a, NT - 2)
                emit_xnTe(a, NT - 1)


        with nc.psum_tensor("ps03", [128, 2048], F32) as ps03:

            def emit_m2(sg):
                for isub in range(4):
                    ii = sg * 4 + isub
                    q2 = isub % 2

                    @block.tensor
                    def _(te, sg=sg, isub=isub, ii=ii, q2=q2):
                        attA = ps03[:, q2 * 1024: q2 * 1024 + 512]
                        attB = ps03[:, q2 * 1024 + 512: q2 * 1024 + 1024]
                        den = ps47[:, 1024 + q2 * 512: 1024 + q2 * 512 + 1]
                        wt(te, E.get(("attev", ii - 2)))
                        wt(te, E[("scpe", sg, 7)])
                        if sg == 0 and isub < 2:
                            wt(te, E[("xnTe", NT - 1)])
                            wt(te, E[("xnw", NT - 1)])
                            wt(te, E["consts"])
                        for jc in range(NT):
                            lhsT = sc_sl(jc, isub * 128, 128)
                            st_ = (jc == 0)
                            sp_ = (jc == NT - 1)
                            te.matmul(attA, lhsT, xntok_sl(jc, 0, 512),
                                      start=st_, stop=sp_)
                            te.matmul(attB, lhsT, xntok_sl(jc, 512, 512),
                                      start=st_, stop=sp_)
                            ins = te.matmul(den, lhsT,
                                            rs_bf[:, jc:jc + 1],
                                            start=st_, stop=sp_)
                        E[("att", ii)] = mark(ins, "pe")

                    @block.vector
                    def _(v, ii=ii, q2=q2):
                        wt(v, E[("att", ii)])
                        den = ps47[:, 1024 + q2 * 512: 1024 + q2 * 512 + 1]
                        E[("recq", ii)] = e5 = mark(
                            v.reciprocal(recd[:, q2:q2 + 1], den), "dv")
                        wt(v, e5)
                        E[("attev", ii)] = mark(v.scalar_tensor_tensor(
                            out=xs(ii), in0=ps03[:, q2 * 1024:
                                                 (q2 + 1) * 1024],
                            scalar=recd[:, q2:q2 + 1], in1=xs(ii),
                            op0=ALU.mult, op1=ALU.add), "dv")
                E[("M2last", sg)] = E[("att", sg * 4 + 3)]

            @block.scalar
            def _(a):
                for t in range(12, 16):
                    wt(a, E[("xncast", t)])
                    wt(a, E[("pcast", t - 4)])
                for t in range(12, 16):
                    tq = t % 4
                    xn32 = scr32[:, tq * D:(tq + 1) * D]
                    E[("exp", t)] = mark(
                        a.activation(xn32, xn32, AF.Exp,
                                     accum_out=st_s[:, t:t + 1]), "ac")

            emit_dve_post(12)
            emit_p_store(12)
            emit_dve_post(13)
            emit_p_store(13)

            for p in range(4):
                emit_m1_pair_te(0, p)
                if p % 2 == 1:
                    emit_m1_evac(0, [p - 1, p])

            emit_dve_post(14)
            emit_p_store(14)
            emit_dve_post(15)
            emit_p_store(15)

            for p in range(4, 8):
                emit_m1_pair_te(0, p)
                if p % 2 == 1:
                    emit_m1_evac(0, [p - 1, p])
            E[("M1last", 0)] = E[("scp", 0, 7)]

            emit_m2(0)
            for sg in range(1, 4):
                for p in range(8):
                    emit_m1_pair_te(sg, p)
                    if p % 2 == 1:
                        emit_m1_evac(sg, [p - 1, p], first=(p == 1))
                E[("M1last", sg)] = E[("scp", sg, 7)]
                emit_m2(sg)

            grp_readers = {0: [E[("attev", 15)]],
                           1: [E[("scpe", 3, 7)], E[("attev", 15)]]}

            @block.sync
            def _(sy):
                sy.wait_ge(sems["ca"], cnt["ca"])
                for b in (0, 1):
                    b2 = b % 2
                    h0 = b * 512
                    sg_ = "g%d" % b2
                    sd_ = "d%d" % b2
                    wt(sy, E[("M1last", 3)])
                    for dc in range(NDC):
                        mark(sy.dma_start(
                            out=wgT_sl(b2, dc * 512, 512),
                            in_=wg_bf[h0:h0 + 512, dc * 128:(dc + 1) * 128],
                            transpose=True), sg_, 16)
                        mark(sy.dma_start(
                            out=wuT_sl(b2, dc * 512, 512),
                            in_=wu_bf[h0:h0 + 512, dc * 128:(dc + 1) * 128],
                            transpose=True), sg_, 16)
                    E[("Wgu", b)] = now(sg_)
                    wt(sy, E[("M2last", 3)])
                    for hcb in range(HBLK):
                        mark(sy.dma_start(
                            out=wdT_sl(b2, hcb * 1024, 1024),
                            in_=wd_bf[:, h0 + hcb * 128: h0 + (hcb + 1) * 128],
                            transpose=True), sd_, 16)
                    E[("Wd", b)] = now(sd_)

            for b in range(NBLK):
                b2 = b % 2
                h0 = b * 512
                last_g0_pass = None

                for hcb in range(HBLK):
                    hc = HBLK * b + hcb
                    for th in range(2):
                        pi = (hc * 2 + th + b) % 2
                        base = ps03 if pi == 0 else ps47
                        idx = hc * 2 + th
                        if pi == 0:
                            last_g0_pass = idx

                        @block.tensor
                        def _(te, b=b, b2=b2, hcb=hcb, th=th, pi=pi,
                              base=base, idx=idx):
                            if hcb == 0 and th == 0:
                                wt(te, E[("Wgu", b)])
                            rd = grp_readers[pi]
                            wt(te, rd[0])
                            for dc in range(NDC):
                                lg = wgT_sl(b2, dc * 512 + hcb * 128, 128)
                                st_ = (dc == 0)
                                sp_ = (dc == NDC - 1)
                                te.matmul(base[:, 0:512], lg,
                                          xnT_sl(dc, th * 1024, 512),
                                          start=st_, stop=sp_)
                                te.matmul(base[:, 512:1024], lg,
                                          xnT_sl(dc, th * 1024 + 512, 512),
                                          start=st_, stop=sp_)
                            for ev in rd[1:]:
                                wt(te, ev)
                            for dc in range(NDC):
                                lu = wuT_sl(b2, dc * 512 + hcb * 128, 128)
                                st_ = (dc == 0)
                                sp_ = (dc == NDC - 1)
                                te.matmul(base[:, 1024:1536], lu,
                                          xnT_sl(dc, th * 1024, 512),
                                          start=st_, stop=sp_)
                                ins = te.matmul(base[:, 1536:2048], lu,
                                                xnT_sl(dc, th * 1024 + 512,
                                                       512),
                                                start=st_, stop=sp_)
                            E[("m3", idx)] = mark(ins, "pe")

                    @block.scalar
                    def _(a, b=b, hc=hc):
                        for th in range(2):
                            idx = hc * 2 + th
                            pi = (idx + b) % 2
                            base = ps03 if pi == 0 else ps47
                            wt(a, E[("m3", idx)])
                            wt(a, E.get(("hact", idx - 2)))
                            E[("silu", idx)] = mark(a.activation(
                                scr32[:, th * 1024:(th + 1) * 1024],
                                base[:, 0:1024], AF.Silu, scale=S127), "ac")

                    for th in range(2):
                        pi = (hc * 2 + th + b) % 2
                        base = ps03 if pi == 0 else ps47
                        idx = hc * 2 + th

                        @block.vector
                        def _(v, b=b, hcb=hcb, th=th, idx=idx, base=base):
                            wt(v, E[("silu", idx)])
                            if b == 0 and hcb == 0 and th == 0:
                                wt(v, E[("attev", 15)])
                            E[("hact", idx)] = mark(v.tensor_tensor(
                                out=hact_sl(hcb * 2048 + th * 1024, 1024),
                                in0=scr32[:, th * 1024:(th + 1) * 1024],
                                in1=base[:, 1024:2048],
                                op=ALU.mult), "dv")

                        grp_readers[pi] = [E[("silu", idx)],
                                           E[("hact", idx)]]

                E[("M3done", b)] = E[("m3", (HBLK * b + HBLK - 1) * 2 + 1)]

                for tt in range(NT):
                    q5 = tt % 2

                    @block.tensor
                    def _(te, b=b, b2=b2, tt=tt, q5=q5,
                          last_g0_pass=last_g0_pass):
                        if tt == 0:
                            wt(te, E[("Wd", b)])
                            wt(te, E[("silu", last_g0_pass)])
                        if tt == 1:
                            wt(te, E[("hact", last_g0_pass)])
                        wt(te, E.get(("sttev", b, tt - 2)))
                        for hcb in range(HBLK):
                            if tt == 0:
                                wt(te, E[("hact", (HBLK * b + hcb) * 2 + 1)])
                            lh = hact_sl(hcb * 2048 + tt * 128, 128)
                            st_ = (hcb == 0)
                            sp_ = (hcb == HBLK - 1)
                            te.matmul(ps03[:, q5 * 1024: q5 * 1024 + 512],
                                      lh, wdT_sl(b2, hcb * 1024, 512),
                                      start=st_, stop=sp_)
                            ins = te.matmul(
                                ps03[:, q5 * 1024 + 512: q5 * 1024 + 1024],
                                lh, wdT_sl(b2, hcb * 1024 + 512, 512),
                                start=st_, stop=sp_)
                        E[("m5", b, tt)] = mark(ins, "pe")

                    @block.vector
                    def _(v, b=b, tt=tt, q5=q5):
                        wt(v, E[("m5", b, tt)])
                        E[("sttev", b, tt)] = mark(v.scalar_tensor_tensor(
                            out=xs(tt),
                            in0=ps03[:, q5 * 1024:(q5 + 1) * 1024],
                            scalar=S127SQ, in1=xs(tt),
                            op0=ALU.mult, op1=ALU.add), "dv")
                E[("M5done", b)] = E[("m5", b, NT - 1)]
                grp_readers[0] = [E[("sttev", b, NT - 2)],
                                  E[("sttev", b, NT - 1)]]

                if b + 2 < NBLK:
                    @block.sync
                    def _(sy, b=b):
                        bn = b + 2
                        b2n = bn % 2
                        h0n = bn * 512
                        sg_ = "g%d" % b2n
                        sd_ = "d%d" % b2n
                        wt(sy, E[("M3done", b)])
                        for dc in range(NDC):
                            mark(sy.dma_start(
                                out=wgT_sl(b2n, dc * 512, 512),
                                in_=wg_bf[h0n:h0n + 512,
                                          dc * 128:(dc + 1) * 128],
                                transpose=True), sg_, 16)
                            mark(sy.dma_start(
                                out=wuT_sl(b2n, dc * 512, 512),
                                in_=wu_bf[h0n:h0n + 512,
                                          dc * 128:(dc + 1) * 128],
                                transpose=True), sg_, 16)
                        E[("Wgu", bn)] = now(sg_)
                        wt(sy, E[("M5done", b)])
                        for hcb in range(HBLK):
                            mark(sy.dma_start(
                                out=wdT_sl(b2n, hcb * 1024, 1024),
                                in_=wd_bf[:, h0n + hcb * 128:
                                          h0n + (hcb + 1) * 128],
                                transpose=True), sd_, 16)
                        E[("Wd", bn)] = now(sd_)

            @block.sync
            def _(sy):
                for tt in range(NT):
                    wt(sy, E[("sttev", NBLK - 1, tt)])
                    mark(sy.dma_start(out=o_r[:, tt, :], in_=xs(tt)),
                         "st", 16)
                sy.wait_ge(sems["st"], cnt["st"])

    return nc


_IDENT = np.eye(128, dtype=ml_dtypes.bfloat16)


def kernel(L0_attn_x, L0_ffn_w_gate, L0_ffn_w_up, L0_ffn_w_down):
    x = np.ascontiguousarray(np.asarray(L0_attn_x, dtype=np.float32))
    wg = np.ascontiguousarray(np.asarray(L0_ffn_w_gate, dtype=np.int32))
    wu = np.ascontiguousarray(np.asarray(L0_ffn_w_up, dtype=np.int32))
    wd = np.ascontiguousarray(np.asarray(L0_ffn_w_down, dtype=np.int32))

    nc = build_nc()
    in_maps = [{"x": x[c], "wg": wg, "wu": wu, "wd": wd, "ident": _IDENT}
               for c in range(8)]
    res = run_bass_kernel_spmd(nc, in_maps, core_ids=list(range(8)))
    return np.stack([np.asarray(r["o"]) for r in res.results], axis=0)


# version 38
# speedup vs baseline: 1.2828x; 1.2828x over previous
import contextlib

import numpy as np
import ml_dtypes
import concourse.bass as bass
import concourse.mybir as mybir
from concourse.bass_utils import run_bass_kernel_spmd

F32 = mybir.dt.float32
BF16 = mybir.dt.bfloat16
I32 = mybir.dt.int32
AF = mybir.ActivationFunctionType
ALU = mybir.AluOpType
AX = mybir.AxisListType

T = 2048
D = 1024
H = 3072
NT = T // 128
NDC = D // 128
NHC = H // 128
HBLK = 4
NBLK = NHC // HBLK
S127 = 1.0 / 127.0
S127SQ = (1.0 / 127.0) ** 2
EPS = 1e-5


def build_nc():
    nc = bass.Bass("TRN2", target_bir_lowering=False)

    x_hbm = nc.dram_tensor("x", [T, D], F32, kind="ExternalInput").ap()
    wg_hbm = nc.dram_tensor("wg", [H, D], I32, kind="ExternalInput").ap()
    wu_hbm = nc.dram_tensor("wu", [H, D], I32, kind="ExternalInput").ap()
    wd_hbm = nc.dram_tensor("wd", [D, H], I32, kind="ExternalInput").ap()
    id_hbm = nc.dram_tensor("ident", [128, 128], BF16, kind="ExternalInput").ap()
    o_hbm = nc.dram_tensor("o", [T, D], F32, kind="ExternalOutput").ap()

    wg_bf = nc.dram_tensor("wg_bf", [H, D], BF16, kind="Internal").ap()
    wu_bf = nc.dram_tensor("wu_bf", [H, D], BF16, kind="Internal").ap()
    wd_bf = nc.dram_tensor("wd_bf", [D, H], BF16, kind="Internal").ap()

    x_r = x_hbm.rearrange("(t p) d -> p t d", p=128)
    o_r = o_hbm.rearrange("(t p) d -> p t d", p=128)

    with contextlib.ExitStack() as ctx:
        def sb(name, shape, dtype):
            return ctx.enter_context(nc.sbuf_tensor(name, shape, dtype))

        x_sb = sb("x_sb", [128, NT * D], F32)
        xnT = sb("xnT", [128, NDC * T], BF16)
        arena = sb("arena", [128, 40960], BF16)
        scr32 = sb("scr32", [128, 4096], F32)
        ptile = sb("ptile", [128, 2 * D], BF16)
        xnb = sb("xnb", [128, 4 * D], BF16)
        sb_id = sb("sb_id", [128, 128], BF16)
        st_eps = sb("st_eps", [128, 1], F32)
        st_bn = sb("st_bn", [128, 24], F32)
        st_mv2 = sb("st_mv2", [128, 2 * NT], F32)
        st_lnv = sb("st_lnv", [128, NT], F32)
        st_rstd = sb("st_rstd", [128, NT], F32)
        st_s = sb("st_s", [128, NT], F32)
        st_rs = sb("st_rs", [128, 2], F32)
        rs_bf = sb("rs_bf", [128, NT], BF16)
        recd = sb("recd", [128, 2], F32)
        s_lx = ctx.enter_context(nc.semaphore("lx"))
        s_ly = ctx.enter_context(nc.semaphore("ly"))
        s_lz = ctx.enter_context(nc.semaphore("lz"))
        s_lv = ctx.enter_context(nc.semaphore("lv"))
        s_l4 = ctx.enter_context(nc.semaphore("l4"))
        s_l5 = ctx.enter_context(nc.semaphore("l5"))
        s_l6 = ctx.enter_context(nc.semaphore("l6"))
        s_l7 = ctx.enter_context(nc.semaphore("l7"))
        s_li = ctx.enter_context(nc.semaphore("li"))
        s_g0 = ctx.enter_context(nc.semaphore("g0"))
        s_g1 = ctx.enter_context(nc.semaphore("g1"))
        s_d0 = ctx.enter_context(nc.semaphore("d0"))
        s_d1 = ctx.enter_context(nc.semaphore("d1"))
        s_ca = ctx.enter_context(nc.semaphore("ca"))
        s_st = ctx.enter_context(nc.semaphore("st"))
        s_pe = ctx.enter_context(nc.semaphore("pe"))
        s_ac = ctx.enter_context(nc.semaphore("ac"))
        s_dv = ctx.enter_context(nc.semaphore("dv"))
        block = ctx.enter_context(nc.Block())
        sems = {"lx": s_lx, "ly": s_ly, "lz": s_lz, "lv": s_lv,
                "l4": s_l4, "l5": s_l5, "l6": s_l6, "l7": s_l7,
                "li": s_li, "g0": s_g0, "g1": s_g1,
                "d0": s_d0, "d1": s_d1, "ca": s_ca, "st": s_st,
                "pe": s_pe, "ac": s_ac, "dv": s_dv}
        cnt = {k: 0 for k in sems}

        def mark(instr, sem, k=1):
            instr.then_inc(sems[sem], k)
            cnt[sem] += k
            return (sem, cnt[sem])

        def wt(eng, ev):
            if ev is not None:
                eng.wait_ge(sems[ev[0]], ev[1])

        def now(sem):
            return (sem, cnt[sem])

        PT0, SC0, XT0 = 0, 16384, 24576

        def pT_sl(dc, i0, n):
            return arena[:, PT0 + dc * T + i0: PT0 + dc * T + i0 + n]

        def sc_sl(jc, i0, n):
            return arena[:, SC0 + jc * 512 + i0: SC0 + jc * 512 + i0 + n]

        def xntok_sl(t, d0, n):
            return arena[:, XT0 + t * D + d0: XT0 + t * D + d0 + n]

        def wgT_sl(b2, c0, n):
            return arena[:, b2 * 4096 + c0: b2 * 4096 + c0 + n]

        def wuT_sl(b2, c0, n):
            return arena[:, 8192 + b2 * 4096 + c0: 8192 + b2 * 4096 + c0 + n]

        def wdT_sl(b2, c0, n):
            return arena[:, 16384 + b2 * 4096 + c0: 16384 + b2 * 4096 + c0 + n]

        def hact_sl(c0, n):
            return arena[:, XT0 + c0: XT0 + c0 + n]

        def xs(t):
            return x_sb[:, t * D:(t + 1) * D]

        def xnT_sl(dc, t0, n):
            return xnT[:, dc * T + t0: dc * T + t0 + n]

        E = {}

        @block.gpsimd
        def _(g):
            mark(g.dma_start(out=sb_id[:, :], in_=id_hbm[:, :]), "li", 16)
            for qi, sm in enumerate(("lx", "ly", "lz", "lv",
                                     "l4", "l5", "l6", "l7")):
                mark(g.dma_start(
                    out=x_sb[:, qi * 2 * D:(qi + 1) * 2 * D].rearrange(
                        "p (t d) -> p t d", t=2),
                    in_=x_r[:, qi * 2:(qi + 1) * 2, :]), sm, 16)
            for r in range(0, H, 768):
                mark(g.dma_start(out=wg_bf[r:r + 768, :],
                                 in_=wg_hbm[r:r + 768, :]), "ca", 16)
                mark(g.dma_start(out=wu_bf[r:r + 768, :],
                                 in_=wu_hbm[r:r + 768, :]), "ca", 16)
            for r in range(0, D, 256):
                mark(g.dma_start(out=wd_bf[r:r + 256, :],
                                 in_=wd_hbm[r:r + 256, :]), "ca", 16)
        E["ident"] = ("li", 16)
        for qi, sm in enumerate(("lx", "ly", "lz", "lv",
                                 "l4", "l5", "l6", "l7")):
            E[("xq8", qi)] = (sm, 16)

        @block.vector
        def _(v):
            mark(v.memset(st_eps[:, :], EPS), "dv")
        E["consts"] = now("dv")

        ps47 = ctx.enter_context(
            nc.psum_tensor("ps47", [128, 2048], F32, side="right"))

        last_pair_evac = {}

        def emit_m1_pair_te(sg, p):
            jcs = (2 * p, 2 * p + 1)
            reg = ps47[:, (p % 2) * 1024: (p % 2) * 1024 + 1024]

            @block.tensor
            def _(te, sg=sg, p=p, jcs=jcs, reg=reg):
                if p < 2:
                    wt(te, E[("pTe", sg * 4 + 3)])
                wt(te, E[("xnTe", 2 * p + 1)])
                wt(te, last_pair_evac.get(p % 2))
                if sg > 0 and p == 1:
                    wt(te, E[("recq", (sg - 1) * 4 + 3)])
                i0 = sg * 512
                for dc in range(NDC):
                    for k, jc in enumerate(jcs):
                        ins = te.matmul(
                            reg[:, k * 512:(k + 1) * 512],
                            xnT_sl(dc, jc * 128, 128),
                            pT_sl(dc, i0, 512),
                            start=(dc == 0), stop=(dc == NDC - 1))
                E[("scp", sg, p)] = mark(ins, "pe")

        def emit_m1_evac(sg, plist, first=False):
            @block.scalar
            def _(a, sg=sg, plist=plist, first=first):
                if first and sg > 0:
                    wt(a, E[("M2last", sg - 1)])
                for p in plist:
                    wt(a, E[("scp", sg, p)])
                    reg = ps47[:, (p % 2) * 1024: (p % 2) * 1024 + 1024]
                    for k, jc in enumerate((2 * p, 2 * p + 1)):
                        ins = a.activation(
                            sc_sl(jc, 0, 512), reg[:, k * 512:(k + 1) * 512],
                            AF.Exp)
                    E[("scpe", sg, p)] = mark(ins, "ac")
                    last_pair_evac[p % 2] = E[("scpe", sg, p)]

        def emit_m1_pair(sg, p):
            emit_m1_pair_te(sg, p)
            emit_m1_evac(sg, [p], first=(p == 0))

        with nc.psum_tensor("ps_bf", [128, 4096], BF16) as ps_bf:
            def emit_pass1(t):
                @block.vector
                def _(v, t=t):
                    qb = (t % 2) * 12
                    wt(v, E[("xq8", t // 2)])
                    wt(v, E.get(("mv", t - 2)))
                    mark(v.bn_stats(st_bn[:, qb:qb + 6], xs(t)[:, 0:512]),
                         "dv")
                    e1 = mark(v.bn_stats(st_bn[:, qb + 6:qb + 12],
                                         xs(t)[:, 512:1024]), "dv")
                    wt(v, e1)
                    E[("mv", t)] = mark(
                        v.bn_aggr(st_mv2[:, 2 * t: 2 * t + 2],
                                  st_bn[:, qb:qb + 12]), "dv")

            var_ap3 = st_mv2[:, :].rearrange("p (t two) -> p t two", two=2)
            def emit_pass2(pair):
                lo = pair * 2
                @block.scalar
                def _(a, lo=lo):
                    wt(a, E[("mv", lo + 1)])
                    e2 = mark(a.activation(
                        st_lnv[:, lo:lo + 2],
                        var_ap3[:, lo:lo + 2, 1:2], AF.Ln,
                        bias=st_eps[:, :]), "ac")
                    wt(a, e2)
                    E[("rstd", lo)] = mark(a.activation(
                        st_rstd[:, lo:lo + 2], st_lnv[:, lo:lo + 2],
                        AF.Exp, scale=-0.5), "ac")

            for t in range(8):
                emit_pass1(t)
            for pr in range(4):
                emit_pass2(pr)

            def emit_xnTe(eng, t):
                qp = t % 2
                dst = xnT[:, :].rearrange("p (dc t) -> p dc t", dc=NDC)[
                    :, :, t * 128:(t + 1) * 128]
                srcx = ps_bf[:, qp * 1024:(qp + 1) * 1024].rearrange(
                    "p (dc t) -> p dc t", dc=NDC)
                if hasattr(eng, "activation"):
                    E[("xnTe", t)] = mark(
                        eng.activation(dst, srcx, AF.Copy), "ac")
                else:
                    E[("xnTe", t)] = mark(
                        eng.tensor_copy(out=dst, in_=srcx), "dv")

            def emit_dve_pre(t):
                @block.vector
                def _(v, t=t):
                    q = t % 2
                    tq = t % 4
                    if t % 2 == 0:
                        wt(v, E[("rstd", t)])
                    xn32 = scr32[:, tq * D:(tq + 1) * D]
                    e3 = mark(v.tensor_scalar(
                        out=xn32, in0=xs(t),
                        scalar1=st_mv2[:, 2 * t: 2 * t + 1],
                        scalar2=st_rstd[:, t:t + 1],
                        op0=ALU.subtract, op1=ALU.mult), "dv")
                    wt(v, e3)
                    wt(v, E.get(("xnTr", t - 4)))
                    E[("xncast", t)] = mark(
                        v.tensor_copy(out=xnb[:, tq * D:(tq + 1) * D],
                                      in_=xn32), "dv")

            def emit_dve_post(t, with_pte=True):
                @block.vector
                def _(v, t=t, with_pte=with_pte):
                    q = t % 2
                    tq = t % 4
                    wt(v, E[("exp", t)])
                    e5 = mark(v.reciprocal(st_rs[:, q:q + 1],
                                           st_s[:, t:t + 1]), "dv")
                    wt(v, e5)
                    wt(v, E.get(("ptr", t - 2)))
                    e32 = scr32[:, tq * D:(tq + 1) * D]
                    E[("pcast", t)] = mark(v.tensor_scalar(
                        out=ptile[:, q * D:(q + 1) * D], in0=e32,
                        scalar1=st_rs[:, q:q + 1], scalar2=None,
                        op0=ALU.mult), "dv")
                    mark(v.tensor_copy(out=rs_bf[:, t:t + 1],
                                       in_=st_rs[:, q:q + 1]), "dv")
                    mark(v.tensor_scalar(
                        out=xntok_sl(t, 0, D),
                        in0=xnb[:, tq * D:(tq + 1) * D],
                        scalar1=st_rs[:, q:q + 1], scalar2=None,
                        op0=ALU.mult), "dv")
                    if t >= 1 and with_pte:
                        qp = (t - 1) % 2
                        wt(v, E[("ptr", t - 1)])
                        dstp = arena[:, PT0:PT0 + NDC * T].rearrange(
                            "p (dc t) -> p dc t", dc=NDC)[
                            :, :, (t - 1) * 128: t * 128]
                        srcp = ps_bf[:, 2048 + qp * 1024:
                                     2048 + (qp + 1) * 1024].rearrange(
                            "p (dc t) -> p dc t", dc=NDC)
                        E[("pTe", t - 1)] = mark(
                            v.tensor_copy(out=dstp, in_=srcp), "dv")

            def emit_pe_transp_xn(t):
                @block.tensor
                def _(te, t=t):
                    q = t % 2
                    tq = t % 4
                    wt(te, E["ident"])
                    wt(te, E[("xncast", t)])
                    wt(te, E.get(("xnTe", t - 2)))
                    for dc in range(NDC):
                        ins = te.transpose(
                            ps_bf[:, q * 1024 + dc * 128:
                                  q * 1024 + dc * 128 + 128],
                            xnb[:, tq * D + dc * 128: tq * D + dc * 128 + 128],
                            sb_id[:, :])
                    E[("xnTr", t)] = mark(ins, "pe")

            def emit_pe_transp_p(t):
                @block.tensor
                def _(te, t=t):
                    q = t % 2
                    wt(te, E[("pcast", t)])
                    wt(te, E.get(("pTe", t - 2)))
                    for dc in range(NDC):
                        ins = te.transpose(
                            ps_bf[:, 2048 + q * 1024 + dc * 128:
                                  2048 + q * 1024 + dc * 128 + 128],
                            ptile[:, q * D + dc * 128: q * D + dc * 128 + 128],
                            sb_id[:, :])
                    E[("ptr", t)] = mark(ins, "pe")

            def emit_pe_transp(t):
                emit_pe_transp_xn(t)
                emit_pe_transp_p(t)

            emit_dve_pre(0)
            emit_dve_pre(1)
            for k in range(NT // 2):
                t0, t1 = 2 * k, 2 * k + 1

                @block.scalar
                def _(a, k=k, t0=t0, t1=t1):
                    wt(a, E[("xncast", t0)])
                    wt(a, E[("xncast", t1)])
                    wt(a, E.get(("pcast", t0 - 4)))
                    wt(a, E.get(("pcast", t1 - 4)))
                    if k >= 1:
                        wt(a, E[("xnTr", t0 - 2)])
                        wt(a, E[("xnTr", t0 - 1)])
                    if k < 6:
                        for t in (t0, t1):
                            tq = t % 4
                            xn32 = scr32[:, tq * D:(tq + 1) * D]
                            E[("exp", t)] = mark(
                                a.activation(xn32, xn32, AF.Exp,
                                             accum_out=st_s[:, t:t + 1]),
                                "ac")
                    if k >= 1:
                        emit_xnTe(a, t0 - 2)
                        emit_xnTe(a, t0 - 1)

                if k < NT // 2 - 1:
                    if k < 4:
                        emit_pass1(8 + 2 * k)
                        emit_pass1(9 + 2 * k)
                        if k == 1:
                            emit_pass2(4)
                            emit_pass2(5)
                        if k == 3:
                            emit_pass2(6)
                            emit_pass2(7)
                    emit_dve_pre(t0 + 2)
                    emit_dve_pre(t1 + 2)
                if k < 6:
                    emit_dve_post(t0)
                    emit_pe_transp(t0)
                    emit_dve_post(t1)
                    emit_pe_transp(t1)
                else:
                    emit_pe_transp_xn(t0)
                    emit_pe_transp_xn(t1)

            @block.scalar
            def _(a):
                wt(a, E[("xnTr", NT - 2)])
                wt(a, E[("xnTr", NT - 1)])
                emit_xnTe(a, NT - 2)
                emit_xnTe(a, NT - 1)


        with nc.psum_tensor("ps03", [128, 2048], F32) as ps03:

            def emit_m2(sg):
                for isub in range(4):
                    ii = sg * 4 + isub
                    q2 = isub % 2

                    @block.tensor
                    def _(te, sg=sg, isub=isub, ii=ii, q2=q2):
                        attA = ps03[:, q2 * 1024: q2 * 1024 + 512]
                        attB = ps03[:, q2 * 1024 + 512: q2 * 1024 + 1024]
                        den = ps47[:, 1024 + q2 * 512: 1024 + q2 * 512 + 1]
                        wt(te, E.get(("attev", ii - 2)))
                        wt(te, E[("scpe", sg, 7)])
                        if sg == 0 and isub < 2:
                            wt(te, E[("xnTe", NT - 1)])
                            wt(te, E[("pTe", NT - 1)])
                            wt(te, E["consts"])
                        for jc in range(NT):
                            lhsT = sc_sl(jc, isub * 128, 128)
                            st_ = (jc == 0)
                            sp_ = (jc == NT - 1)
                            te.matmul(attA, lhsT, xntok_sl(jc, 0, 512),
                                      start=st_, stop=sp_)
                            te.matmul(attB, lhsT, xntok_sl(jc, 512, 512),
                                      start=st_, stop=sp_)
                            ins = te.matmul(den, lhsT,
                                            rs_bf[:, jc:jc + 1],
                                            start=st_, stop=sp_)
                        E[("att", ii)] = mark(ins, "pe")

                    @block.vector
                    def _(v, ii=ii, q2=q2):
                        wt(v, E[("att", ii)])
                        den = ps47[:, 1024 + q2 * 512: 1024 + q2 * 512 + 1]
                        E[("recq", ii)] = e5 = mark(
                            v.reciprocal(recd[:, q2:q2 + 1], den), "dv")
                        wt(v, e5)
                        E[("attev", ii)] = mark(v.scalar_tensor_tensor(
                            out=xs(ii), in0=ps03[:, q2 * 1024:
                                                 (q2 + 1) * 1024],
                            scalar=recd[:, q2:q2 + 1], in1=xs(ii),
                            op0=ALU.mult, op1=ALU.add), "dv")
                E[("M2last", sg)] = E[("att", sg * 4 + 3)]

            @block.scalar
            def _(a):
                for t in range(12, 16):
                    wt(a, E[("xncast", t)])
                    wt(a, E[("pcast", t - 4)])
                for t in range(12, 16):
                    tq = t % 4
                    xn32 = scr32[:, tq * D:(tq + 1) * D]
                    E[("exp", t)] = mark(
                        a.activation(xn32, xn32, AF.Exp,
                                     accum_out=st_s[:, t:t + 1]), "ac")

            emit_dve_post(12, with_pte=True)
            emit_dve_post(13, with_pte=False)

            for p in range(4):
                emit_m1_pair_te(0, p)
                if p % 2 == 1:
                    emit_m1_evac(0, [p - 1, p])

            emit_pe_transp_p(12)
            emit_pe_transp_p(13)

            @block.vector
            def _(v):
                for t in (12, 13):
                    qp = t % 2
                    wt(v, E[("ptr", t)])
                    dstp = arena[:, PT0:PT0 + NDC * T].rearrange(
                        "p (dc t) -> p dc t", dc=NDC)[
                        :, :, t * 128:(t + 1) * 128]
                    srcp = ps_bf[:, 2048 + qp * 1024: 2048 + (qp + 1) * 1024
                                 ].rearrange("p (dc t) -> p dc t", dc=NDC)
                    E[("pTe", t)] = mark(
                        v.tensor_copy(out=dstp, in_=srcp), "dv")

            emit_dve_post(14, with_pte=False)
            emit_dve_post(15, with_pte=False)

            for p in range(4, 8):
                emit_m1_pair_te(0, p)
                if p % 2 == 1:
                    emit_m1_evac(0, [p - 1, p])
            E[("M1last", 0)] = E[("scp", 0, 7)]

            emit_pe_transp_p(14)
            emit_pe_transp_p(15)

            @block.vector
            def _(v):
                for t in (14, 15):
                    qp = t % 2
                    wt(v, E[("ptr", t)])
                    dstp = arena[:, PT0:PT0 + NDC * T].rearrange(
                        "p (dc t) -> p dc t", dc=NDC)[
                        :, :, t * 128:(t + 1) * 128]
                    srcp = ps_bf[:, 2048 + qp * 1024: 2048 + (qp + 1) * 1024
                                 ].rearrange("p (dc t) -> p dc t", dc=NDC)
                    E[("pTe", t)] = mark(
                        v.tensor_copy(out=dstp, in_=srcp), "dv")

            emit_m2(0)
            for sg in range(1, 4):
                for p in range(8):
                    emit_m1_pair_te(sg, p)
                    if p % 2 == 1:
                        emit_m1_evac(sg, [p - 1, p], first=(p == 1))
                E[("M1last", sg)] = E[("scp", sg, 7)]
                emit_m2(sg)

            grp_readers = {0: [E[("attev", 15)]],
                           1: [E[("scpe", 3, 7)], E[("attev", 15)]]}

            @block.sync
            def _(sy):
                sy.wait_ge(sems["ca"], cnt["ca"])
                for b in (0, 1):
                    b2 = b % 2
                    h0 = b * 512
                    sg_ = "g%d" % b2
                    sd_ = "d%d" % b2
                    wt(sy, E[("M1last", 3)])
                    for dc in range(NDC):
                        mark(sy.dma_start(
                            out=wgT_sl(b2, dc * 512, 512),
                            in_=wg_bf[h0:h0 + 512, dc * 128:(dc + 1) * 128],
                            transpose=True), sg_, 16)
                        mark(sy.dma_start(
                            out=wuT_sl(b2, dc * 512, 512),
                            in_=wu_bf[h0:h0 + 512, dc * 128:(dc + 1) * 128],
                            transpose=True), sg_, 16)
                    E[("Wgu", b)] = now(sg_)
                    wt(sy, E[("M2last", 3)])
                    for hcb in range(HBLK):
                        mark(sy.dma_start(
                            out=wdT_sl(b2, hcb * 1024, 1024),
                            in_=wd_bf[:, h0 + hcb * 128: h0 + (hcb + 1) * 128],
                            transpose=True), sd_, 16)
                    E[("Wd", b)] = now(sd_)

            for b in range(NBLK):
                b2 = b % 2
                h0 = b * 512
                last_g0_pass = None

                for hcb in range(HBLK):
                    hc = HBLK * b + hcb
                    for th in range(2):
                        pi = (hc * 2 + th + b) % 2
                        base = ps03 if pi == 0 else ps47
                        idx = hc * 2 + th
                        if pi == 0:
                            last_g0_pass = idx

                        @block.tensor
                        def _(te, b=b, b2=b2, hcb=hcb, th=th, pi=pi,
                              base=base, idx=idx):
                            if hcb == 0 and th == 0:
                                wt(te, E[("Wgu", b)])
                            rd = grp_readers[pi]
                            wt(te, rd[0])
                            for dc in range(NDC):
                                lg = wgT_sl(b2, dc * 512 + hcb * 128, 128)
                                st_ = (dc == 0)
                                sp_ = (dc == NDC - 1)
                                te.matmul(base[:, 0:512], lg,
                                          xnT_sl(dc, th * 1024, 512),
                                          start=st_, stop=sp_)
                                te.matmul(base[:, 512:1024], lg,
                                          xnT_sl(dc, th * 1024 + 512, 512),
                                          start=st_, stop=sp_)
                            for ev in rd[1:]:
                                wt(te, ev)
                            for dc in range(NDC):
                                lu = wuT_sl(b2, dc * 512 + hcb * 128, 128)
                                st_ = (dc == 0)
                                sp_ = (dc == NDC - 1)
                                te.matmul(base[:, 1024:1536], lu,
                                          xnT_sl(dc, th * 1024, 512),
                                          start=st_, stop=sp_)
                                ins = te.matmul(base[:, 1536:2048], lu,
                                                xnT_sl(dc, th * 1024 + 512,
                                                       512),
                                                start=st_, stop=sp_)
                            E[("m3", idx)] = mark(ins, "pe")

                    @block.scalar
                    def _(a, b=b, hc=hc):
                        for th in range(2):
                            idx = hc * 2 + th
                            pi = (idx + b) % 2
                            base = ps03 if pi == 0 else ps47
                            wt(a, E[("m3", idx)])
                            wt(a, E.get(("hact", idx - 2)))
                            E[("silu", idx)] = mark(a.activation(
                                scr32[:, th * 1024:(th + 1) * 1024],
                                base[:, 0:1024], AF.Silu, scale=S127), "ac")

                    for th in range(2):
                        pi = (hc * 2 + th + b) % 2
                        base = ps03 if pi == 0 else ps47
                        idx = hc * 2 + th

                        @block.vector
                        def _(v, b=b, hcb=hcb, th=th, idx=idx, base=base):
                            wt(v, E[("silu", idx)])
                            if b == 0 and hcb == 0 and th == 0:
                                wt(v, E[("attev", 15)])
                            E[("hact", idx)] = mark(v.tensor_tensor(
                                out=hact_sl(hcb * 2048 + th * 1024, 1024),
                                in0=scr32[:, th * 1024:(th + 1) * 1024],
                                in1=base[:, 1024:2048],
                                op=ALU.mult), "dv")

                        grp_readers[pi] = [E[("silu", idx)],
                                           E[("hact", idx)]]

                E[("M3done", b)] = E[("m3", (HBLK * b + HBLK - 1) * 2 + 1)]

                for tt in range(NT):
                    q5 = tt % 2

                    @block.tensor
                    def _(te, b=b, b2=b2, tt=tt, q5=q5,
                          last_g0_pass=last_g0_pass):
                        if tt == 0:
                            wt(te, E[("Wd", b)])
                            wt(te, E[("silu", last_g0_pass)])
                        if tt == 1:
                            wt(te, E[("hact", last_g0_pass)])
                        wt(te, E.get(("sttev", b, tt - 2)))
                        for hcb in range(HBLK):
                            if tt == 0:
                                wt(te, E[("hact", (HBLK * b + hcb) * 2 + 1)])
                            lh = hact_sl(hcb * 2048 + tt * 128, 128)
                            st_ = (hcb == 0)
                            sp_ = (hcb == HBLK - 1)
                            te.matmul(ps03[:, q5 * 1024: q5 * 1024 + 512],
                                      lh, wdT_sl(b2, hcb * 1024, 512),
                                      start=st_, stop=sp_)
                            ins = te.matmul(
                                ps03[:, q5 * 1024 + 512: q5 * 1024 + 1024],
                                lh, wdT_sl(b2, hcb * 1024 + 512, 512),
                                start=st_, stop=sp_)
                        E[("m5", b, tt)] = mark(ins, "pe")

                    @block.vector
                    def _(v, b=b, tt=tt, q5=q5):
                        wt(v, E[("m5", b, tt)])
                        E[("sttev", b, tt)] = mark(v.scalar_tensor_tensor(
                            out=xs(tt),
                            in0=ps03[:, q5 * 1024:(q5 + 1) * 1024],
                            scalar=S127SQ, in1=xs(tt),
                            op0=ALU.mult, op1=ALU.add), "dv")
                E[("M5done", b)] = E[("m5", b, NT - 1)]
                grp_readers[0] = [E[("sttev", b, NT - 2)],
                                  E[("sttev", b, NT - 1)]]

                if b + 2 < NBLK:
                    @block.sync
                    def _(sy, b=b):
                        bn = b + 2
                        b2n = bn % 2
                        h0n = bn * 512
                        sg_ = "g%d" % b2n
                        sd_ = "d%d" % b2n
                        wt(sy, E[("M3done", b)])
                        for dc in range(NDC):
                            mark(sy.dma_start(
                                out=wgT_sl(b2n, dc * 512, 512),
                                in_=wg_bf[h0n:h0n + 512,
                                          dc * 128:(dc + 1) * 128],
                                transpose=True), sg_, 16)
                            mark(sy.dma_start(
                                out=wuT_sl(b2n, dc * 512, 512),
                                in_=wu_bf[h0n:h0n + 512,
                                          dc * 128:(dc + 1) * 128],
                                transpose=True), sg_, 16)
                        E[("Wgu", bn)] = now(sg_)
                        wt(sy, E[("M5done", b)])
                        for hcb in range(HBLK):
                            mark(sy.dma_start(
                                out=wdT_sl(b2n, hcb * 1024, 1024),
                                in_=wd_bf[:, h0n + hcb * 128:
                                          h0n + (hcb + 1) * 128],
                                transpose=True), sd_, 16)
                        E[("Wd", bn)] = now(sd_)

            @block.sync
            def _(sy):
                for tt in range(NT):
                    wt(sy, E[("sttev", NBLK - 1, tt)])
                    mark(sy.dma_start(out=o_r[:, tt, :], in_=xs(tt)),
                         "st", 16)
                sy.wait_ge(sems["st"], cnt["st"])

    return nc


_IDENT = np.eye(128, dtype=ml_dtypes.bfloat16)


def kernel(L0_attn_x, L0_ffn_w_gate, L0_ffn_w_up, L0_ffn_w_down):
    x = np.ascontiguousarray(np.asarray(L0_attn_x, dtype=np.float32))
    wg = np.ascontiguousarray(np.asarray(L0_ffn_w_gate, dtype=np.int32))
    wu = np.ascontiguousarray(np.asarray(L0_ffn_w_up, dtype=np.int32))
    wd = np.ascontiguousarray(np.asarray(L0_ffn_w_down, dtype=np.int32))

    nc = build_nc()
    in_maps = [{"x": x[c], "wg": wg, "wu": wu, "wd": wd, "ident": _IDENT}
               for c in range(8)]
    res = run_bass_kernel_spmd(nc, in_maps, core_ids=list(range(8)))
    return np.stack([np.asarray(r["o"]) for r in res.results], axis=0)


# version 39
# speedup vs baseline: 1.2879x; 1.0040x over previous
import contextlib

import numpy as np
import ml_dtypes
import concourse.bass as bass
import concourse.mybir as mybir
from concourse.bass_utils import run_bass_kernel_spmd

F32 = mybir.dt.float32
BF16 = mybir.dt.bfloat16
I32 = mybir.dt.int32
AF = mybir.ActivationFunctionType
ALU = mybir.AluOpType
AX = mybir.AxisListType

T = 2048
D = 1024
H = 3072
NT = T // 128
NDC = D // 128
NHC = H // 128
HBLK = 4
NBLK = NHC // HBLK
S127 = 1.0 / 127.0
S127SQ = (1.0 / 127.0) ** 2
EPS = 1e-5


def build_nc():
    nc = bass.Bass("TRN2", target_bir_lowering=False)

    x_hbm = nc.dram_tensor("x", [T, D], F32, kind="ExternalInput").ap()
    wg_hbm = nc.dram_tensor("wg", [H, D], I32, kind="ExternalInput").ap()
    wu_hbm = nc.dram_tensor("wu", [H, D], I32, kind="ExternalInput").ap()
    wd_hbm = nc.dram_tensor("wd", [D, H], I32, kind="ExternalInput").ap()
    id_hbm = nc.dram_tensor("ident", [128, 128], BF16, kind="ExternalInput").ap()
    o_hbm = nc.dram_tensor("o", [T, D], F32, kind="ExternalOutput").ap()

    wg_bf = nc.dram_tensor("wg_bf", [H, D], BF16, kind="Internal").ap()
    wu_bf = nc.dram_tensor("wu_bf", [H, D], BF16, kind="Internal").ap()
    wd_bf = nc.dram_tensor("wd_bf", [D, H], BF16, kind="Internal").ap()

    x_r = x_hbm.rearrange("(t p) d -> p t d", p=128)
    o_r = o_hbm.rearrange("(t p) d -> p t d", p=128)

    with contextlib.ExitStack() as ctx:
        def sb(name, shape, dtype):
            return ctx.enter_context(nc.sbuf_tensor(name, shape, dtype))

        x_sb = sb("x_sb", [128, NT * D], F32)
        xnT = sb("xnT", [128, NDC * T], BF16)
        arena = sb("arena", [128, 40960], BF16)
        scr32 = sb("scr32", [128, 4096], F32)
        ptile = sb("ptile", [128, 2 * D], BF16)
        xnb = sb("xnb", [128, 4 * D], BF16)
        sb_id = sb("sb_id", [128, 128], BF16)
        st_eps = sb("st_eps", [128, 1], F32)
        st_bn = sb("st_bn", [128, 24], F32)
        st_mv2 = sb("st_mv2", [128, 2 * NT], F32)
        st_lnv = sb("st_lnv", [128, NT], F32)
        st_rstd = sb("st_rstd", [128, NT], F32)
        st_s = sb("st_s", [128, NT], F32)
        st_rs = sb("st_rs", [128, 2], F32)
        rs_bf = sb("rs_bf", [128, NT], BF16)
        recd = sb("recd", [128, 2], F32)
        s_lx = ctx.enter_context(nc.semaphore("lx"))
        s_ly = ctx.enter_context(nc.semaphore("ly"))
        s_lz = ctx.enter_context(nc.semaphore("lz"))
        s_lv = ctx.enter_context(nc.semaphore("lv"))
        s_l4 = ctx.enter_context(nc.semaphore("l4"))
        s_l5 = ctx.enter_context(nc.semaphore("l5"))
        s_l6 = ctx.enter_context(nc.semaphore("l6"))
        s_l7 = ctx.enter_context(nc.semaphore("l7"))
        s_li = ctx.enter_context(nc.semaphore("li"))
        s_g0 = ctx.enter_context(nc.semaphore("g0"))
        s_g1 = ctx.enter_context(nc.semaphore("g1"))
        s_d0 = ctx.enter_context(nc.semaphore("d0"))
        s_d1 = ctx.enter_context(nc.semaphore("d1"))
        s_ca = ctx.enter_context(nc.semaphore("ca"))
        s_st = ctx.enter_context(nc.semaphore("st"))
        s_pe = ctx.enter_context(nc.semaphore("pe"))
        s_ac = ctx.enter_context(nc.semaphore("ac"))
        s_dv = ctx.enter_context(nc.semaphore("dv"))
        block = ctx.enter_context(nc.Block())
        sems = {"lx": s_lx, "ly": s_ly, "lz": s_lz, "lv": s_lv,
                "l4": s_l4, "l5": s_l5, "l6": s_l6, "l7": s_l7,
                "li": s_li, "g0": s_g0, "g1": s_g1,
                "d0": s_d0, "d1": s_d1, "ca": s_ca, "st": s_st,
                "pe": s_pe, "ac": s_ac, "dv": s_dv}
        cnt = {k: 0 for k in sems}

        def mark(instr, sem, k=1):
            instr.then_inc(sems[sem], k)
            cnt[sem] += k
            return (sem, cnt[sem])

        def wt(eng, ev):
            if ev is not None:
                eng.wait_ge(sems[ev[0]], ev[1])

        def now(sem):
            return (sem, cnt[sem])

        PT0, SC0, XT0 = 0, 16384, 24576

        def pT_sl(dc, i0, n):
            return arena[:, PT0 + dc * T + i0: PT0 + dc * T + i0 + n]

        def sc_sl(jc, i0, n):
            return arena[:, SC0 + jc * 512 + i0: SC0 + jc * 512 + i0 + n]

        def xntok_sl(t, d0, n):
            return arena[:, XT0 + t * D + d0: XT0 + t * D + d0 + n]

        def wgT_sl(b2, c0, n):
            return arena[:, b2 * 4096 + c0: b2 * 4096 + c0 + n]

        def wuT_sl(b2, c0, n):
            return arena[:, 8192 + b2 * 4096 + c0: 8192 + b2 * 4096 + c0 + n]

        def wdT_sl(b2, c0, n):
            return arena[:, 16384 + b2 * 4096 + c0: 16384 + b2 * 4096 + c0 + n]

        def hact_sl(c0, n):
            return arena[:, XT0 + c0: XT0 + c0 + n]

        def xs(t):
            return x_sb[:, t * D:(t + 1) * D]

        def xnT_sl(dc, t0, n):
            return xnT[:, dc * T + t0: dc * T + t0 + n]

        E = {}

        @block.gpsimd
        def _(g):
            mark(g.dma_start(out=sb_id[:, :], in_=id_hbm[:, :]), "li", 16)
            for qi, sm in enumerate(("lx", "ly", "lz", "lv",
                                     "l4", "l5", "l6", "l7")):
                mark(g.dma_start(
                    out=x_sb[:, qi * 2 * D:(qi + 1) * 2 * D].rearrange(
                        "p (t d) -> p t d", t=2),
                    in_=x_r[:, qi * 2:(qi + 1) * 2, :]), sm, 16)
            for r in range(0, H, 768):
                mark(g.dma_start(out=wg_bf[r:r + 768, :],
                                 in_=wg_hbm[r:r + 768, :]), "ca", 16)
                mark(g.dma_start(out=wu_bf[r:r + 768, :],
                                 in_=wu_hbm[r:r + 768, :]), "ca", 16)
            for r in range(0, D, 256):
                mark(g.dma_start(out=wd_bf[r:r + 256, :],
                                 in_=wd_hbm[r:r + 256, :]), "ca", 16)
        E["ident"] = ("li", 16)
        for qi, sm in enumerate(("lx", "ly", "lz", "lv",
                                 "l4", "l5", "l6", "l7")):
            E[("xq8", qi)] = (sm, 16)

        @block.vector
        def _(v):
            mark(v.memset(st_eps[:, :], EPS), "dv")
        E["consts"] = now("dv")

        ps47 = ctx.enter_context(
            nc.psum_tensor("ps47", [128, 2048], F32, side="right"))

        last_pair_evac = {}

        def emit_m1_pair_te(sg, p):
            jcs = (2 * p, 2 * p + 1)
            reg = ps47[:, (p % 2) * 1024: (p % 2) * 1024 + 1024]

            @block.tensor
            def _(te, sg=sg, p=p, jcs=jcs, reg=reg):
                if p < 2:
                    wt(te, E[("pTe", sg * 4 + 3)])
                wt(te, E[("xnTe", 2 * p + 1)])
                wt(te, last_pair_evac.get(p % 2))
                if sg > 0 and p == 0:
                    wt(te, E[("recq", (sg - 1) * 4 + 3)])
                i0 = sg * 512
                for dc in range(NDC):
                    for k, jc in enumerate(jcs):
                        ins = te.matmul(
                            reg[:, k * 512:(k + 1) * 512],
                            xnT_sl(dc, jc * 128, 128),
                            pT_sl(dc, i0, 512),
                            start=(dc == 0), stop=(dc == NDC - 1))
                E[("scp", sg, p)] = mark(ins, "pe")

        def emit_m1_evac(sg, plist, first=False):
            @block.scalar
            def _(a, sg=sg, plist=plist, first=first):
                if first and sg > 0:
                    wt(a, E[("M2last", sg - 1)])
                for p in plist:
                    wt(a, E[("scp", sg, p)])
                    reg = ps47[:, (p % 2) * 1024: (p % 2) * 1024 + 1024]
                    for k, jc in enumerate((2 * p, 2 * p + 1)):
                        ins = a.activation(
                            sc_sl(jc, 0, 512), reg[:, k * 512:(k + 1) * 512],
                            AF.Exp)
                    E[("scpe", sg, p)] = mark(ins, "ac")
                    last_pair_evac[p % 2] = E[("scpe", sg, p)]

        def emit_m1_pair(sg, p):
            emit_m1_pair_te(sg, p)
            emit_m1_evac(sg, [p], first=(p == 0))

        with nc.psum_tensor("ps_bf", [128, 4096], BF16) as ps_bf:
            def emit_pass1(t):
                @block.vector
                def _(v, t=t):
                    qb = (t % 2) * 12
                    wt(v, E[("xq8", t // 2)])
                    wt(v, E.get(("mv", t - 2)))
                    mark(v.bn_stats(st_bn[:, qb:qb + 6], xs(t)[:, 0:512]),
                         "dv")
                    e1 = mark(v.bn_stats(st_bn[:, qb + 6:qb + 12],
                                         xs(t)[:, 512:1024]), "dv")
                    wt(v, e1)
                    E[("mv", t)] = mark(
                        v.bn_aggr(st_mv2[:, 2 * t: 2 * t + 2],
                                  st_bn[:, qb:qb + 12]), "dv")

            var_ap3 = st_mv2[:, :].rearrange("p (t two) -> p t two", two=2)
            def emit_pass2(pair):
                lo = pair * 2
                @block.scalar
                def _(a, lo=lo):
                    wt(a, E[("mv", lo + 1)])
                    e2 = mark(a.activation(
                        st_lnv[:, lo:lo + 2],
                        var_ap3[:, lo:lo + 2, 1:2], AF.Ln,
                        bias=st_eps[:, :]), "ac")
                    wt(a, e2)
                    E[("rstd", lo)] = mark(a.activation(
                        st_rstd[:, lo:lo + 2], st_lnv[:, lo:lo + 2],
                        AF.Exp, scale=-0.5), "ac")

            for t in range(8):
                emit_pass1(t)
            for pr in range(4):
                emit_pass2(pr)

            def emit_xnTe(eng, t):
                qp = t % 2
                dst = xnT[:, :].rearrange("p (dc t) -> p dc t", dc=NDC)[
                    :, :, t * 128:(t + 1) * 128]
                srcx = ps_bf[:, qp * 1024:(qp + 1) * 1024].rearrange(
                    "p (dc t) -> p dc t", dc=NDC)
                if hasattr(eng, "activation"):
                    E[("xnTe", t)] = mark(
                        eng.activation(dst, srcx, AF.Copy), "ac")
                else:
                    E[("xnTe", t)] = mark(
                        eng.tensor_copy(out=dst, in_=srcx), "dv")

            def emit_dve_pre(t):
                @block.vector
                def _(v, t=t):
                    q = t % 2
                    tq = t % 4
                    if t % 2 == 0:
                        wt(v, E[("rstd", t)])
                    xn32 = scr32[:, tq * D:(tq + 1) * D]
                    e3 = mark(v.tensor_scalar(
                        out=xn32, in0=xs(t),
                        scalar1=st_mv2[:, 2 * t: 2 * t + 1],
                        scalar2=st_rstd[:, t:t + 1],
                        op0=ALU.subtract, op1=ALU.mult), "dv")
                    wt(v, e3)
                    wt(v, E.get(("xnTr", t - 4)))
                    E[("xncast", t)] = mark(
                        v.tensor_copy(out=xnb[:, tq * D:(tq + 1) * D],
                                      in_=xn32), "dv")

            def emit_dve_post(t, with_pte=True):
                @block.vector
                def _(v, t=t, with_pte=with_pte):
                    q = t % 2
                    tq = t % 4
                    wt(v, E[("exp", t)])
                    e5 = mark(v.reciprocal(st_rs[:, q:q + 1],
                                           st_s[:, t:t + 1]), "dv")
                    wt(v, e5)
                    wt(v, E.get(("ptr", t - 2)))
                    e32 = scr32[:, tq * D:(tq + 1) * D]
                    E[("pcast", t)] = mark(v.tensor_scalar(
                        out=ptile[:, q * D:(q + 1) * D], in0=e32,
                        scalar1=st_rs[:, q:q + 1], scalar2=None,
                        op0=ALU.mult), "dv")
                    mark(v.tensor_copy(out=rs_bf[:, t:t + 1],
                                       in_=st_rs[:, q:q + 1]), "dv")
                    mark(v.tensor_scalar(
                        out=xntok_sl(t, 0, D),
                        in0=xnb[:, tq * D:(tq + 1) * D],
                        scalar1=st_rs[:, q:q + 1], scalar2=None,
                        op0=ALU.mult), "dv")
                    if t >= 1 and with_pte:
                        qp = (t - 1) % 2
                        wt(v, E[("ptr", t - 1)])
                        dstp = arena[:, PT0:PT0 + NDC * T].rearrange(
                            "p (dc t) -> p dc t", dc=NDC)[
                            :, :, (t - 1) * 128: t * 128]
                        srcp = ps_bf[:, 2048 + qp * 1024:
                                     2048 + (qp + 1) * 1024].rearrange(
                            "p (dc t) -> p dc t", dc=NDC)
                        E[("pTe", t - 1)] = mark(
                            v.tensor_copy(out=dstp, in_=srcp), "dv")

            def emit_pe_transp_xn(t):
                @block.tensor
                def _(te, t=t):
                    q = t % 2
                    tq = t % 4
                    wt(te, E["ident"])
                    wt(te, E[("xncast", t)])
                    wt(te, E.get(("xnTe", t - 2)))
                    for dc in range(NDC):
                        ins = te.transpose(
                            ps_bf[:, q * 1024 + dc * 128:
                                  q * 1024 + dc * 128 + 128],
                            xnb[:, tq * D + dc * 128: tq * D + dc * 128 + 128],
                            sb_id[:, :])
                    E[("xnTr", t)] = mark(ins, "pe")

            def emit_pe_transp_p(t):
                @block.tensor
                def _(te, t=t):
                    q = t % 2
                    wt(te, E[("pcast", t)])
                    wt(te, E.get(("pTe", t - 2)))
                    for dc in range(NDC):
                        ins = te.transpose(
                            ps_bf[:, 2048 + q * 1024 + dc * 128:
                                  2048 + q * 1024 + dc * 128 + 128],
                            ptile[:, q * D + dc * 128: q * D + dc * 128 + 128],
                            sb_id[:, :])
                    E[("ptr", t)] = mark(ins, "pe")

            def emit_pe_transp(t):
                emit_pe_transp_xn(t)
                emit_pe_transp_p(t)

            emit_dve_pre(0)
            emit_dve_pre(1)
            for k in range(NT // 2):
                t0, t1 = 2 * k, 2 * k + 1

                @block.scalar
                def _(a, k=k, t0=t0, t1=t1):
                    wt(a, E[("xncast", t0)])
                    wt(a, E[("xncast", t1)])
                    wt(a, E.get(("pcast", t0 - 4)))
                    wt(a, E.get(("pcast", t1 - 4)))
                    if k >= 1:
                        wt(a, E[("xnTr", t0 - 2)])
                        wt(a, E[("xnTr", t0 - 1)])
                    if k < 6:
                        for t in (t0, t1):
                            tq = t % 4
                            xn32 = scr32[:, tq * D:(tq + 1) * D]
                            E[("exp", t)] = mark(
                                a.activation(xn32, xn32, AF.Exp,
                                             accum_out=st_s[:, t:t + 1]),
                                "ac")
                    if k >= 1:
                        emit_xnTe(a, t0 - 2)
                        emit_xnTe(a, t0 - 1)

                if k < NT // 2 - 1:
                    if k < 4:
                        emit_pass1(8 + 2 * k)
                        emit_pass1(9 + 2 * k)
                        if k == 1:
                            emit_pass2(4)
                            emit_pass2(5)
                        if k == 3:
                            emit_pass2(6)
                            emit_pass2(7)
                    emit_dve_pre(t0 + 2)
                    emit_dve_pre(t1 + 2)
                if k < 6:
                    emit_dve_post(t0)
                    emit_pe_transp(t0)
                    emit_dve_post(t1)
                    emit_pe_transp(t1)
                else:
                    emit_pe_transp_xn(t0)
                    emit_pe_transp_xn(t1)

            @block.scalar
            def _(a):
                wt(a, E[("xnTr", NT - 2)])
                wt(a, E[("xnTr", NT - 1)])
                emit_xnTe(a, NT - 2)
                emit_xnTe(a, NT - 1)


        with nc.psum_tensor("ps03", [128, 2048], F32) as ps03:

            def emit_m2(sg):
                for isub in range(4):
                    ii = sg * 4 + isub
                    q2 = isub % 2

                    @block.tensor
                    def _(te, sg=sg, isub=isub, ii=ii, q2=q2):
                        attA = ps03[:, q2 * 1024: q2 * 1024 + 512]
                        attB = ps03[:, q2 * 1024 + 512: q2 * 1024 + 1024]
                        den = ps47[:, q2 * 512: q2 * 512 + 1]
                        wt(te, E.get(("attev", ii - 2)))
                        wt(te, E[("scpe", sg, 6)])
                        if sg == 0 and isub < 2:
                            wt(te, E[("xnTe", NT - 1)])
                            wt(te, E[("pTe", NT - 1)])
                            wt(te, E["consts"])
                        for jc in range(NT):
                            if jc == 14:
                                wt(te, E[("scpe", sg, 7)])
                            lhsT = sc_sl(jc, isub * 128, 128)
                            st_ = (jc == 0)
                            sp_ = (jc == NT - 1)
                            te.matmul(attA, lhsT, xntok_sl(jc, 0, 512),
                                      start=st_, stop=sp_)
                            te.matmul(attB, lhsT, xntok_sl(jc, 512, 512),
                                      start=st_, stop=sp_)
                            ins = te.matmul(den, lhsT,
                                            rs_bf[:, jc:jc + 1],
                                            start=st_, stop=sp_)
                        E[("att", ii)] = mark(ins, "pe")

                    @block.vector
                    def _(v, ii=ii, q2=q2):
                        wt(v, E[("att", ii)])
                        den = ps47[:, q2 * 512: q2 * 512 + 1]
                        E[("recq", ii)] = e5 = mark(
                            v.reciprocal(recd[:, q2:q2 + 1], den), "dv")
                        wt(v, e5)
                        E[("attev", ii)] = mark(v.scalar_tensor_tensor(
                            out=xs(ii), in0=ps03[:, q2 * 1024:
                                                 (q2 + 1) * 1024],
                            scalar=recd[:, q2:q2 + 1], in1=xs(ii),
                            op0=ALU.mult, op1=ALU.add), "dv")
                E[("M2last", sg)] = E[("att", sg * 4 + 3)]

            @block.scalar
            def _(a):
                for t in range(12, 16):
                    wt(a, E[("xncast", t)])
                    wt(a, E[("pcast", t - 4)])
                for t in range(12, 16):
                    tq = t % 4
                    xn32 = scr32[:, tq * D:(tq + 1) * D]
                    E[("exp", t)] = mark(
                        a.activation(xn32, xn32, AF.Exp,
                                     accum_out=st_s[:, t:t + 1]), "ac")

            emit_dve_post(12, with_pte=True)
            emit_dve_post(13, with_pte=False)

            for p in range(4):
                emit_m1_pair_te(0, p)
                if p % 2 == 1:
                    emit_m1_evac(0, [p - 1, p])

            emit_pe_transp_p(12)
            emit_pe_transp_p(13)

            @block.vector
            def _(v):
                for t in (12, 13):
                    qp = t % 2
                    wt(v, E[("ptr", t)])
                    dstp = arena[:, PT0:PT0 + NDC * T].rearrange(
                        "p (dc t) -> p dc t", dc=NDC)[
                        :, :, t * 128:(t + 1) * 128]
                    srcp = ps_bf[:, 2048 + qp * 1024: 2048 + (qp + 1) * 1024
                                 ].rearrange("p (dc t) -> p dc t", dc=NDC)
                    E[("pTe", t)] = mark(
                        v.tensor_copy(out=dstp, in_=srcp), "dv")

            emit_dve_post(14, with_pte=False)
            emit_dve_post(15, with_pte=False)

            for p in range(4, 8):
                emit_m1_pair_te(0, p)
                if p % 2 == 1:
                    emit_m1_evac(0, [p - 1, p])
            E[("M1last", 0)] = E[("scp", 0, 7)]

            emit_pe_transp_p(14)
            emit_pe_transp_p(15)

            @block.vector
            def _(v):
                for t in (14, 15):
                    qp = t % 2
                    wt(v, E[("ptr", t)])
                    dstp = arena[:, PT0:PT0 + NDC * T].rearrange(
                        "p (dc t) -> p dc t", dc=NDC)[
                        :, :, t * 128:(t + 1) * 128]
                    srcp = ps_bf[:, 2048 + qp * 1024: 2048 + (qp + 1) * 1024
                                 ].rearrange("p (dc t) -> p dc t", dc=NDC)
                    E[("pTe", t)] = mark(
                        v.tensor_copy(out=dstp, in_=srcp), "dv")

            emit_m2(0)
            for sg in range(1, 4):
                for p in range(8):
                    emit_m1_pair_te(sg, p)
                    if p % 2 == 1:
                        emit_m1_evac(sg, [p - 1, p], first=(p == 1))
                E[("M1last", sg)] = E[("scp", sg, 7)]
                emit_m2(sg)

            grp_readers = {0: [E[("attev", 15)]],
                           1: [E[("scpe", 3, 7)], E[("attev", 15)]]}

            @block.sync
            def _(sy):
                sy.wait_ge(sems["ca"], cnt["ca"])
                for b in (0, 1):
                    b2 = b % 2
                    h0 = b * 512
                    sg_ = "g%d" % b2
                    sd_ = "d%d" % b2
                    wt(sy, E[("M1last", 3)])
                    for dc in range(NDC):
                        mark(sy.dma_start(
                            out=wgT_sl(b2, dc * 512, 512),
                            in_=wg_bf[h0:h0 + 512, dc * 128:(dc + 1) * 128],
                            transpose=True), sg_, 16)
                        mark(sy.dma_start(
                            out=wuT_sl(b2, dc * 512, 512),
                            in_=wu_bf[h0:h0 + 512, dc * 128:(dc + 1) * 128],
                            transpose=True), sg_, 16)
                    E[("Wgu", b)] = now(sg_)
                    wt(sy, E[("M2last", 3)])
                    for hcb in range(HBLK):
                        mark(sy.dma_start(
                            out=wdT_sl(b2, hcb * 1024, 1024),
                            in_=wd_bf[:, h0 + hcb * 128: h0 + (hcb + 1) * 128],
                            transpose=True), sd_, 16)
                    E[("Wd", b)] = now(sd_)

            for b in range(NBLK):
                b2 = b % 2
                h0 = b * 512
                last_g0_pass = None

                for hcb in range(HBLK):
                    hc = HBLK * b + hcb
                    for th in range(2):
                        pi = (hc * 2 + th + b) % 2
                        base = ps03 if pi == 0 else ps47
                        idx = hc * 2 + th
                        if pi == 0:
                            last_g0_pass = idx

                        @block.tensor
                        def _(te, b=b, b2=b2, hcb=hcb, th=th, pi=pi,
                              base=base, idx=idx):
                            if hcb == 0 and th == 0:
                                wt(te, E[("Wgu", b)])
                            rd = grp_readers[pi]
                            wt(te, rd[0])
                            for dc in range(NDC):
                                lg = wgT_sl(b2, dc * 512 + hcb * 128, 128)
                                st_ = (dc == 0)
                                sp_ = (dc == NDC - 1)
                                te.matmul(base[:, 0:512], lg,
                                          xnT_sl(dc, th * 1024, 512),
                                          start=st_, stop=sp_)
                                te.matmul(base[:, 512:1024], lg,
                                          xnT_sl(dc, th * 1024 + 512, 512),
                                          start=st_, stop=sp_)
                            for ev in rd[1:]:
                                wt(te, ev)
                            for dc in range(NDC):
                                lu = wuT_sl(b2, dc * 512 + hcb * 128, 128)
                                st_ = (dc == 0)
                                sp_ = (dc == NDC - 1)
                                te.matmul(base[:, 1024:1536], lu,
                                          xnT_sl(dc, th * 1024, 512),
                                          start=st_, stop=sp_)
                                ins = te.matmul(base[:, 1536:2048], lu,
                                                xnT_sl(dc, th * 1024 + 512,
                                                       512),
                                                start=st_, stop=sp_)
                            E[("m3", idx)] = mark(ins, "pe")

                    @block.scalar
                    def _(a, b=b, hc=hc):
                        for th in range(2):
                            idx = hc * 2 + th
                            pi = (idx + b) % 2
                            base = ps03 if pi == 0 else ps47
                            wt(a, E[("m3", idx)])
                            wt(a, E.get(("hact", idx - 2)))
                            E[("silu", idx)] = mark(a.activation(
                                scr32[:, th * 1024:(th + 1) * 1024],
                                base[:, 0:1024], AF.Silu, scale=S127), "ac")

                    for th in range(2):
                        pi = (hc * 2 + th + b) % 2
                        base = ps03 if pi == 0 else ps47
                        idx = hc * 2 + th

                        @block.vector
                        def _(v, b=b, hcb=hcb, th=th, idx=idx, base=base):
                            wt(v, E[("silu", idx)])
                            if b == 0 and hcb == 0 and th == 0:
                                wt(v, E[("attev", 15)])
                            E[("hact", idx)] = mark(v.tensor_tensor(
                                out=hact_sl(hcb * 2048 + th * 1024, 1024),
                                in0=scr32[:, th * 1024:(th + 1) * 1024],
                                in1=base[:, 1024:2048],
                                op=ALU.mult), "dv")

                        grp_readers[pi] = [E[("silu", idx)],
                                           E[("hact", idx)]]

                E[("M3done", b)] = E[("m3", (HBLK * b + HBLK - 1) * 2 + 1)]

                for tt in range(NT):
                    q5 = tt % 2

                    @block.tensor
                    def _(te, b=b, b2=b2, tt=tt, q5=q5,
                          last_g0_pass=last_g0_pass):
                        if tt == 0:
                            wt(te, E[("Wd", b)])
                            wt(te, E[("silu", last_g0_pass)])
                        if tt == 1:
                            wt(te, E[("hact", last_g0_pass)])
                        wt(te, E.get(("sttev", b, tt - 2)))
                        for hcb in range(HBLK):
                            if tt == 0:
                                wt(te, E[("hact", (HBLK * b + hcb) * 2 + 1)])
                            lh = hact_sl(hcb * 2048 + tt * 128, 128)
                            st_ = (hcb == 0)
                            sp_ = (hcb == HBLK - 1)
                            te.matmul(ps03[:, q5 * 1024: q5 * 1024 + 512],
                                      lh, wdT_sl(b2, hcb * 1024, 512),
                                      start=st_, stop=sp_)
                            ins = te.matmul(
                                ps03[:, q5 * 1024 + 512: q5 * 1024 + 1024],
                                lh, wdT_sl(b2, hcb * 1024 + 512, 512),
                                start=st_, stop=sp_)
                        E[("m5", b, tt)] = mark(ins, "pe")

                    @block.vector
                    def _(v, b=b, tt=tt, q5=q5):
                        wt(v, E[("m5", b, tt)])
                        E[("sttev", b, tt)] = mark(v.scalar_tensor_tensor(
                            out=xs(tt),
                            in0=ps03[:, q5 * 1024:(q5 + 1) * 1024],
                            scalar=S127SQ, in1=xs(tt),
                            op0=ALU.mult, op1=ALU.add), "dv")
                E[("M5done", b)] = E[("m5", b, NT - 1)]
                grp_readers[0] = [E[("sttev", b, NT - 2)],
                                  E[("sttev", b, NT - 1)]]

                if b + 2 < NBLK:
                    @block.sync
                    def _(sy, b=b):
                        bn = b + 2
                        b2n = bn % 2
                        h0n = bn * 512
                        sg_ = "g%d" % b2n
                        sd_ = "d%d" % b2n
                        wt(sy, E[("M3done", b)])
                        for dc in range(NDC):
                            mark(sy.dma_start(
                                out=wgT_sl(b2n, dc * 512, 512),
                                in_=wg_bf[h0n:h0n + 512,
                                          dc * 128:(dc + 1) * 128],
                                transpose=True), sg_, 16)
                            mark(sy.dma_start(
                                out=wuT_sl(b2n, dc * 512, 512),
                                in_=wu_bf[h0n:h0n + 512,
                                          dc * 128:(dc + 1) * 128],
                                transpose=True), sg_, 16)
                        E[("Wgu", bn)] = now(sg_)
                        wt(sy, E[("M5done", b)])
                        for hcb in range(HBLK):
                            mark(sy.dma_start(
                                out=wdT_sl(b2n, hcb * 1024, 1024),
                                in_=wd_bf[:, h0n + hcb * 128:
                                          h0n + (hcb + 1) * 128],
                                transpose=True), sd_, 16)
                        E[("Wd", bn)] = now(sd_)

            @block.sync
            def _(sy):
                for tt in range(NT):
                    wt(sy, E[("sttev", NBLK - 1, tt)])
                    mark(sy.dma_start(out=o_r[:, tt, :], in_=xs(tt)),
                         "st", 16)
                sy.wait_ge(sems["st"], cnt["st"])

    return nc


_IDENT = np.eye(128, dtype=ml_dtypes.bfloat16)


def kernel(L0_attn_x, L0_ffn_w_gate, L0_ffn_w_up, L0_ffn_w_down):
    x = np.ascontiguousarray(np.asarray(L0_attn_x, dtype=np.float32))
    wg = np.ascontiguousarray(np.asarray(L0_ffn_w_gate, dtype=np.int32))
    wu = np.ascontiguousarray(np.asarray(L0_ffn_w_up, dtype=np.int32))
    wd = np.ascontiguousarray(np.asarray(L0_ffn_w_down, dtype=np.int32))

    nc = build_nc()
    in_maps = [{"x": x[c], "wg": wg, "wu": wu, "wd": wd, "ident": _IDENT}
               for c in range(8)]
    res = run_bass_kernel_spmd(nc, in_maps, core_ids=list(range(8)))
    return np.stack([np.asarray(r["o"]) for r in res.results], axis=0)


# version 40
# speedup vs baseline: 1.2999x; 1.0093x over previous
import contextlib

import numpy as np
import ml_dtypes
import concourse.bass as bass
import concourse.mybir as mybir
from concourse.bass_utils import run_bass_kernel_spmd

F32 = mybir.dt.float32
BF16 = mybir.dt.bfloat16
I32 = mybir.dt.int32
AF = mybir.ActivationFunctionType
ALU = mybir.AluOpType
AX = mybir.AxisListType

T = 2048
D = 1024
H = 3072
NT = T // 128
NDC = D // 128
NHC = H // 128
HBLK = 4
NBLK = NHC // HBLK
S127 = 1.0 / 127.0
S127SQ = (1.0 / 127.0) ** 2
EPS = 1e-5


def build_nc():
    nc = bass.Bass("TRN2", target_bir_lowering=False)

    x_hbm = nc.dram_tensor("x", [T, D], F32, kind="ExternalInput").ap()
    wg_hbm = nc.dram_tensor("wg", [H, D], I32, kind="ExternalInput").ap()
    wu_hbm = nc.dram_tensor("wu", [H, D], I32, kind="ExternalInput").ap()
    wd_hbm = nc.dram_tensor("wd", [D, H], I32, kind="ExternalInput").ap()
    id_hbm = nc.dram_tensor("ident", [128, 128], BF16, kind="ExternalInput").ap()
    o_hbm = nc.dram_tensor("o", [T, D], F32, kind="ExternalOutput").ap()

    wg_bf = nc.dram_tensor("wg_bf", [H, D], BF16, kind="Internal").ap()
    wu_bf = nc.dram_tensor("wu_bf", [H, D], BF16, kind="Internal").ap()
    wd_bf = nc.dram_tensor("wd_bf", [D, H], BF16, kind="Internal").ap()

    x_r = x_hbm.rearrange("(t p) d -> p t d", p=128)
    o_r = o_hbm.rearrange("(t p) d -> p t d", p=128)

    with contextlib.ExitStack() as ctx:
        def sb(name, shape, dtype):
            return ctx.enter_context(nc.sbuf_tensor(name, shape, dtype))

        x_sb = sb("x_sb", [128, NT * D], F32)
        xnT = sb("xnT", [128, NDC * T], BF16)
        arena = sb("arena", [128, 40960], BF16)
        scr32 = sb("scr32", [128, 4096], F32)
        ptile = sb("ptile", [128, 2 * D], BF16)
        xnb = sb("xnb", [128, 4 * D], BF16)
        sb_id = sb("sb_id", [128, 128], BF16)
        st_eps = sb("st_eps", [128, 1], F32)
        st_bn = sb("st_bn", [128, 24], F32)
        st_mv2 = sb("st_mv2", [128, 2 * NT], F32)
        st_lnv = sb("st_lnv", [128, NT], F32)
        st_rstd = sb("st_rstd", [128, NT], F32)
        st_s = sb("st_s", [128, NT], F32)
        st_rs = sb("st_rs", [128, 2], F32)
        rs_bf = sb("rs_bf", [128, NT], BF16)
        recd = sb("recd", [128, 2], F32)
        s_lx = ctx.enter_context(nc.semaphore("lx"))
        s_ly = ctx.enter_context(nc.semaphore("ly"))
        s_lz = ctx.enter_context(nc.semaphore("lz"))
        s_lv = ctx.enter_context(nc.semaphore("lv"))
        s_l4 = ctx.enter_context(nc.semaphore("l4"))
        s_l5 = ctx.enter_context(nc.semaphore("l5"))
        s_l6 = ctx.enter_context(nc.semaphore("l6"))
        s_l7 = ctx.enter_context(nc.semaphore("l7"))
        s_li = ctx.enter_context(nc.semaphore("li"))
        s_g0 = ctx.enter_context(nc.semaphore("g0"))
        s_g1 = ctx.enter_context(nc.semaphore("g1"))
        s_d0 = ctx.enter_context(nc.semaphore("d0"))
        s_d1 = ctx.enter_context(nc.semaphore("d1"))
        s_ca = ctx.enter_context(nc.semaphore("ca"))
        s_st = ctx.enter_context(nc.semaphore("st"))
        s_pe = ctx.enter_context(nc.semaphore("pe"))
        s_ac = ctx.enter_context(nc.semaphore("ac"))
        s_dv = ctx.enter_context(nc.semaphore("dv"))
        block = ctx.enter_context(nc.Block())
        sems = {"lx": s_lx, "ly": s_ly, "lz": s_lz, "lv": s_lv,
                "l4": s_l4, "l5": s_l5, "l6": s_l6, "l7": s_l7,
                "li": s_li, "g0": s_g0, "g1": s_g1,
                "d0": s_d0, "d1": s_d1, "ca": s_ca, "st": s_st,
                "pe": s_pe, "ac": s_ac, "dv": s_dv}
        cnt = {k: 0 for k in sems}

        def mark(instr, sem, k=1):
            instr.then_inc(sems[sem], k)
            cnt[sem] += k
            return (sem, cnt[sem])

        def wt(eng, ev):
            if ev is not None:
                eng.wait_ge(sems[ev[0]], ev[1])

        def now(sem):
            return (sem, cnt[sem])

        PT0, SC0, XT0 = 0, 16384, 24576

        def pT_sl(dc, i0, n):
            return arena[:, PT0 + dc * T + i0: PT0 + dc * T + i0 + n]

        def sc_sl(jc, i0, n):
            return arena[:, SC0 + jc * 512 + i0: SC0 + jc * 512 + i0 + n]

        def xntok_sl(t, d0, n):
            return arena[:, XT0 + t * D + d0: XT0 + t * D + d0 + n]

        def wgT_sl(b2, c0, n):
            return arena[:, b2 * 4096 + c0: b2 * 4096 + c0 + n]

        def wuT_sl(b2, c0, n):
            return arena[:, 8192 + b2 * 4096 + c0: 8192 + b2 * 4096 + c0 + n]

        def wdT_sl(b2, c0, n):
            return arena[:, 16384 + b2 * 4096 + c0: 16384 + b2 * 4096 + c0 + n]

        def hact_sl(c0, n):
            return arena[:, XT0 + c0: XT0 + c0 + n]

        def xs(t):
            return x_sb[:, t * D:(t + 1) * D]

        def xnT_sl(dc, t0, n):
            return xnT[:, dc * T + t0: dc * T + t0 + n]

        E = {}

        @block.gpsimd
        def _(g):
            mark(g.dma_start(out=sb_id[:, :], in_=id_hbm[:, :]), "li", 16)
            for qi, sm in enumerate(("lx", "ly", "lz", "lv",
                                     "l4", "l5", "l6", "l7")):
                mark(g.dma_start(
                    out=x_sb[:, qi * 2 * D:(qi + 1) * 2 * D].rearrange(
                        "p (t d) -> p t d", t=2),
                    in_=x_r[:, qi * 2:(qi + 1) * 2, :]), sm, 16)
            for r in range(0, H, 768):
                mark(g.dma_start(out=wg_bf[r:r + 768, :],
                                 in_=wg_hbm[r:r + 768, :]), "ca", 16)
                mark(g.dma_start(out=wu_bf[r:r + 768, :],
                                 in_=wu_hbm[r:r + 768, :]), "ca", 16)
            for r in range(0, D, 256):
                mark(g.dma_start(out=wd_bf[r:r + 256, :],
                                 in_=wd_hbm[r:r + 256, :]), "ca", 16)
        E["ident"] = ("li", 16)
        for qi, sm in enumerate(("lx", "ly", "lz", "lv",
                                 "l4", "l5", "l6", "l7")):
            E[("xq8", qi)] = (sm, 16)

        @block.vector
        def _(v):
            mark(v.memset(st_eps[:, :], EPS), "dv")
        E["consts"] = now("dv")

        ps47 = ctx.enter_context(
            nc.psum_tensor("ps47", [128, 2048], F32, side="right"))

        last_pair_evac = {}

        def emit_m1_pair_te(sg, p):
            jcs = (2 * p, 2 * p + 1)
            reg = ps47[:, (p % 2) * 1024: (p % 2) * 1024 + 1024]

            @block.tensor
            def _(te, sg=sg, p=p, jcs=jcs, reg=reg):
                if p < 2:
                    wt(te, E[("pTe", sg * 4 + 3)])
                wt(te, E[("xnTe", 2 * p + 1)])
                wt(te, last_pair_evac.get(p % 2))
                if sg > 0 and p == 0:
                    wt(te, E[("recq", (sg - 1) * 4 + 3)])
                i0 = sg * 512
                for dc in range(NDC):
                    for k, jc in enumerate(jcs):
                        ins = te.matmul(
                            reg[:, k * 512:(k + 1) * 512],
                            xnT_sl(dc, jc * 128, 128),
                            pT_sl(dc, i0, 512),
                            start=(dc == 0), stop=(dc == NDC - 1))
                E[("scp", sg, p)] = mark(ins, "pe")

        def emit_m1_evac(sg, plist, first=False):
            @block.scalar
            def _(a, sg=sg, plist=plist, first=first):
                if first and sg > 0:
                    wt(a, E[("M2last", sg - 1)])
                for p in plist:
                    wt(a, E[("scp", sg, p)])
                    reg = ps47[:, (p % 2) * 1024: (p % 2) * 1024 + 1024]
                    for k, jc in enumerate((2 * p, 2 * p + 1)):
                        ins = a.activation(
                            sc_sl(jc, 0, 512), reg[:, k * 512:(k + 1) * 512],
                            AF.Exp)
                    E[("scpe", sg, p)] = mark(ins, "ac")
                    last_pair_evac[p % 2] = E[("scpe", sg, p)]

        def emit_m1_pair(sg, p):
            emit_m1_pair_te(sg, p)
            emit_m1_evac(sg, [p], first=(p == 0))

        with nc.psum_tensor("ps_bf", [128, 4096], BF16) as ps_bf:
            def emit_pass1(t):
                @block.vector
                def _(v, t=t):
                    qb = (t % 2) * 12
                    wt(v, E[("xq8", t // 2)])
                    wt(v, E.get(("mv", t - 2)))
                    mark(v.bn_stats(st_bn[:, qb:qb + 6], xs(t)[:, 0:512]),
                         "dv")
                    e1 = mark(v.bn_stats(st_bn[:, qb + 6:qb + 12],
                                         xs(t)[:, 512:1024]), "dv")
                    wt(v, e1)
                    E[("mv", t)] = mark(
                        v.bn_aggr(st_mv2[:, 2 * t: 2 * t + 2],
                                  st_bn[:, qb:qb + 12]), "dv")

            var_ap3 = st_mv2[:, :].rearrange("p (t two) -> p t two", two=2)
            def emit_pass2(pair):
                lo = pair * 2
                @block.scalar
                def _(a, lo=lo):
                    wt(a, E[("mv", lo + 1)])
                    e2 = mark(a.activation(
                        st_lnv[:, lo:lo + 2],
                        var_ap3[:, lo:lo + 2, 1:2], AF.Ln,
                        bias=st_eps[:, :]), "ac")
                    wt(a, e2)
                    E[("rstd", lo)] = mark(a.activation(
                        st_rstd[:, lo:lo + 2], st_lnv[:, lo:lo + 2],
                        AF.Exp, scale=-0.5), "ac")

            for t in range(8):
                emit_pass1(t)
            for pr in range(4):
                emit_pass2(pr)

            def emit_xnTe(eng, t):
                qp = t % 2
                dst = xnT[:, :].rearrange("p (dc t) -> p dc t", dc=NDC)[
                    :, :, t * 128:(t + 1) * 128]
                srcx = ps_bf[:, qp * 1024:(qp + 1) * 1024].rearrange(
                    "p (dc t) -> p dc t", dc=NDC)
                if hasattr(eng, "activation"):
                    E[("xnTe", t)] = mark(
                        eng.activation(dst, srcx, AF.Copy), "ac")
                else:
                    E[("xnTe", t)] = mark(
                        eng.tensor_copy(out=dst, in_=srcx), "dv")

            def emit_dve_pre(t):
                @block.vector
                def _(v, t=t):
                    q = t % 2
                    tq = t % 4
                    if t % 2 == 0:
                        wt(v, E[("rstd", t)])
                    xn32 = scr32[:, tq * D:(tq + 1) * D]
                    e3 = mark(v.tensor_scalar(
                        out=xn32, in0=xs(t),
                        scalar1=st_mv2[:, 2 * t: 2 * t + 1],
                        scalar2=st_rstd[:, t:t + 1],
                        op0=ALU.subtract, op1=ALU.mult), "dv")
                    wt(v, e3)
                    wt(v, E.get(("xnTr", t - 4)))
                    E[("xncast", t)] = mark(
                        v.tensor_copy(out=xnb[:, tq * D:(tq + 1) * D],
                                      in_=xn32), "dv")

            def emit_dve_post(t, with_pte=True):
                @block.vector
                def _(v, t=t, with_pte=with_pte):
                    q = t % 2
                    tq = t % 4
                    wt(v, E[("exp", t)])
                    e5 = mark(v.reciprocal(st_rs[:, q:q + 1],
                                           st_s[:, t:t + 1]), "dv")
                    wt(v, e5)
                    wt(v, E.get(("ptr", t - 2)))
                    e32 = scr32[:, tq * D:(tq + 1) * D]
                    E[("pcast", t)] = mark(v.tensor_scalar(
                        out=ptile[:, q * D:(q + 1) * D], in0=e32,
                        scalar1=st_rs[:, q:q + 1], scalar2=None,
                        op0=ALU.mult), "dv")
                    mark(v.tensor_copy(out=rs_bf[:, t:t + 1],
                                       in_=st_rs[:, q:q + 1]), "dv")
                    mark(v.tensor_scalar(
                        out=xntok_sl(t, 0, D),
                        in0=xnb[:, tq * D:(tq + 1) * D],
                        scalar1=st_rs[:, q:q + 1], scalar2=None,
                        op0=ALU.mult), "dv")
                    if t >= 1 and with_pte:
                        qp = (t - 1) % 2
                        wt(v, E[("ptr", t - 1)])
                        dstp = arena[:, PT0:PT0 + NDC * T].rearrange(
                            "p (dc t) -> p dc t", dc=NDC)[
                            :, :, (t - 1) * 128: t * 128]
                        srcp = ps_bf[:, 2048 + qp * 1024:
                                     2048 + (qp + 1) * 1024].rearrange(
                            "p (dc t) -> p dc t", dc=NDC)
                        E[("pTe", t - 1)] = mark(
                            v.tensor_copy(out=dstp, in_=srcp), "dv")

            def emit_pe_transp_xn(t):
                @block.tensor
                def _(te, t=t):
                    q = t % 2
                    tq = t % 4
                    wt(te, E["ident"])
                    wt(te, E[("xncast", t)])
                    wt(te, E.get(("xnTe", t - 2)))
                    for dc in range(NDC):
                        ins = te.transpose(
                            ps_bf[:, q * 1024 + dc * 128:
                                  q * 1024 + dc * 128 + 128],
                            xnb[:, tq * D + dc * 128: tq * D + dc * 128 + 128],
                            sb_id[:, :])
                    E[("xnTr", t)] = mark(ins, "pe")

            def emit_pe_transp_p(t):
                @block.tensor
                def _(te, t=t):
                    q = t % 2
                    wt(te, E[("pcast", t)])
                    wt(te, E.get(("pTe", t - 2)))
                    for dc in range(NDC):
                        ins = te.transpose(
                            ps_bf[:, 2048 + q * 1024 + dc * 128:
                                  2048 + q * 1024 + dc * 128 + 128],
                            ptile[:, q * D + dc * 128: q * D + dc * 128 + 128],
                            sb_id[:, :])
                    E[("ptr", t)] = mark(ins, "pe")

            def emit_pe_transp(t):
                emit_pe_transp_xn(t)
                emit_pe_transp_p(t)

            emit_dve_pre(0)
            emit_dve_pre(1)
            for k in range(NT // 2):
                t0, t1 = 2 * k, 2 * k + 1

                @block.scalar
                def _(a, k=k, t0=t0, t1=t1):
                    wt(a, E[("xncast", t0)])
                    wt(a, E[("xncast", t1)])
                    wt(a, E.get(("pcast", t0 - 4)))
                    wt(a, E.get(("pcast", t1 - 4)))
                    if k >= 1:
                        wt(a, E[("xnTr", t0 - 2)])
                        wt(a, E[("xnTr", t0 - 1)])
                    if k < 6:
                        for t in (t0, t1):
                            tq = t % 4
                            xn32 = scr32[:, tq * D:(tq + 1) * D]
                            E[("exp", t)] = mark(
                                a.activation(xn32, xn32, AF.Exp,
                                             accum_out=st_s[:, t:t + 1]),
                                "ac")
                    if k >= 1:
                        emit_xnTe(a, t0 - 2)
                        emit_xnTe(a, t0 - 1)

                if k < NT // 2 - 1:
                    if k < 4:
                        emit_pass1(8 + 2 * k)
                        emit_pass1(9 + 2 * k)
                        if k == 1:
                            emit_pass2(4)
                            emit_pass2(5)
                        if k == 3:
                            emit_pass2(6)
                            emit_pass2(7)
                    emit_dve_pre(t0 + 2)
                    emit_dve_pre(t1 + 2)
                if k < 6:
                    emit_dve_post(t0)
                    emit_pe_transp(t0)
                    emit_dve_post(t1)
                    emit_pe_transp(t1)
                else:
                    emit_pe_transp_xn(t0)
                    emit_pe_transp_xn(t1)

            @block.scalar
            def _(a):
                wt(a, E[("xnTr", NT - 2)])
                wt(a, E[("xnTr", NT - 1)])
                emit_xnTe(a, NT - 2)
                emit_xnTe(a, NT - 1)


        with nc.psum_tensor("ps03", [128, 2048], F32) as ps03:

            def emit_m2(sg):
                for isub in range(4):
                    ii = sg * 4 + isub
                    q2 = isub % 2

                    @block.tensor
                    def _(te, sg=sg, isub=isub, ii=ii, q2=q2):
                        attA = ps03[:, q2 * 1024: q2 * 1024 + 512]
                        attB = ps03[:, q2 * 1024 + 512: q2 * 1024 + 1024]
                        den = ps47[:, q2 * 512: q2 * 512 + 1]
                        wt(te, E.get(("attev", ii - 2)))
                        wt(te, E[("scpe", sg, 6)])
                        if sg == 0 and isub < 2:
                            wt(te, E[("xnTe", NT - 1)])
                            wt(te, E[("pTe", NT - 1)])
                            wt(te, E["consts"])
                        for jc in range(NT):
                            if jc == 14:
                                wt(te, E[("scpe", sg, 7)])
                            lhsT = sc_sl(jc, isub * 128, 128)
                            st_ = (jc == 0)
                            sp_ = (jc == NT - 1)
                            te.matmul(attA, lhsT, xntok_sl(jc, 0, 512),
                                      start=st_, stop=sp_)
                            te.matmul(attB, lhsT, xntok_sl(jc, 512, 512),
                                      start=st_, stop=sp_)
                            ins = te.matmul(den, lhsT,
                                            rs_bf[:, jc:jc + 1],
                                            start=st_, stop=sp_)
                        E[("att", ii)] = mark(ins, "pe")

                    @block.vector
                    def _(v, ii=ii, q2=q2):
                        wt(v, E[("att", ii)])
                        den = ps47[:, q2 * 512: q2 * 512 + 1]
                        E[("recq", ii)] = e5 = mark(
                            v.reciprocal(recd[:, q2:q2 + 1], den), "dv")
                        wt(v, e5)
                        E[("attev", ii)] = mark(v.scalar_tensor_tensor(
                            out=xs(ii), in0=ps03[:, q2 * 1024:
                                                 (q2 + 1) * 1024],
                            scalar=recd[:, q2:q2 + 1], in1=xs(ii),
                            op0=ALU.mult, op1=ALU.add), "dv")
                E[("M2last", sg)] = E[("att", sg * 4 + 3)]

            @block.scalar
            def _(a):
                for t in range(12, 16):
                    wt(a, E[("xncast", t)])
                    wt(a, E[("pcast", t - 4)])
                for t in range(12, 16):
                    tq = t % 4
                    xn32 = scr32[:, tq * D:(tq + 1) * D]
                    E[("exp", t)] = mark(
                        a.activation(xn32, xn32, AF.Exp,
                                     accum_out=st_s[:, t:t + 1]), "ac")

            emit_dve_post(12, with_pte=True)
            emit_dve_post(13, with_pte=False)

            for p in range(4):
                emit_m1_pair_te(0, p)
                if p % 2 == 1:
                    emit_m1_evac(0, [p - 1, p])

            emit_pe_transp_p(12)
            emit_pe_transp_p(13)

            @block.vector
            def _(v):
                for t in (12, 13):
                    qp = t % 2
                    wt(v, E[("ptr", t)])
                    dstp = arena[:, PT0:PT0 + NDC * T].rearrange(
                        "p (dc t) -> p dc t", dc=NDC)[
                        :, :, t * 128:(t + 1) * 128]
                    srcp = ps_bf[:, 2048 + qp * 1024: 2048 + (qp + 1) * 1024
                                 ].rearrange("p (dc t) -> p dc t", dc=NDC)
                    E[("pTe", t)] = mark(
                        v.tensor_copy(out=dstp, in_=srcp), "dv")

            emit_dve_post(14, with_pte=False)
            emit_dve_post(15, with_pte=False)

            for p in range(4, 8):
                emit_m1_pair_te(0, p)
                if p % 2 == 1:
                    emit_m1_evac(0, [p - 1, p])
            E[("M1last", 0)] = E[("scp", 0, 7)]

            emit_pe_transp_p(14)
            emit_pe_transp_p(15)

            @block.vector
            def _(v):
                for t in (14, 15):
                    qp = t % 2
                    wt(v, E[("ptr", t)])
                    dstp = arena[:, PT0:PT0 + NDC * T].rearrange(
                        "p (dc t) -> p dc t", dc=NDC)[
                        :, :, t * 128:(t + 1) * 128]
                    srcp = ps_bf[:, 2048 + qp * 1024: 2048 + (qp + 1) * 1024
                                 ].rearrange("p (dc t) -> p dc t", dc=NDC)
                    E[("pTe", t)] = mark(
                        v.tensor_copy(out=dstp, in_=srcp), "dv")

            emit_m2(0)
            for sg in range(1, 4):
                for p in range(8):
                    emit_m1_pair_te(sg, p)
                    if p % 2 == 1:
                        emit_m1_evac(sg, [p - 1, p], first=(p == 1))
                E[("M1last", sg)] = E[("scp", sg, 7)]
                emit_m2(sg)

            grp_readers = {0: [E[("attev", 15)]],
                           1: [E[("scpe", 3, 7)], E[("attev", 15)]]}

            @block.sync
            def _(sy):
                sy.wait_ge(sems["ca"], cnt["ca"])
                for b in (0, 1):
                    b2 = b % 2
                    h0 = b * 512
                    sg_ = "g%d" % b2
                    sd_ = "d%d" % b2
                    wt(sy, E[("M1last", 3)])
                    for dc in range(NDC):
                        mark(sy.dma_start(
                            out=wgT_sl(b2, dc * 512, 512),
                            in_=wg_bf[h0:h0 + 512, dc * 128:(dc + 1) * 128],
                            transpose=True), sg_, 16)
                        mark(sy.dma_start(
                            out=wuT_sl(b2, dc * 512, 512),
                            in_=wu_bf[h0:h0 + 512, dc * 128:(dc + 1) * 128],
                            transpose=True), sg_, 16)
                    E[("Wgu", b)] = now(sg_)
                    wt(sy, E[("M2last", 3)])
                    for hcb in range(HBLK):
                        mark(sy.dma_start(
                            out=wdT_sl(b2, hcb * 1024, 1024),
                            in_=wd_bf[:, h0 + hcb * 128: h0 + (hcb + 1) * 128],
                            transpose=True), sd_, 16)
                    E[("Wd", b)] = now(sd_)

            for b in range(NBLK):
                b2 = b % 2
                h0 = b * 512
                last_g0_pass = None

                for hcb in range(HBLK):
                    hc = HBLK * b + hcb
                    for th in range(2):
                        pi = (hc * 2 + th + b) % 2
                        base = ps03 if pi == 0 else ps47
                        idx = hc * 2 + th
                        if pi == 0:
                            last_g0_pass = idx

                        @block.tensor
                        def _(te, b=b, b2=b2, hcb=hcb, th=th, pi=pi,
                              base=base, idx=idx):
                            if hcb == 0 and th == 0:
                                wt(te, E[("Wgu", b)])
                            rd = grp_readers[pi]
                            wt(te, rd[0])
                            for dc in range(NDC):
                                lg = wgT_sl(b2, dc * 512 + hcb * 128, 128)
                                st_ = (dc == 0)
                                sp_ = (dc == NDC - 1)
                                te.matmul(base[:, 0:512], lg,
                                          xnT_sl(dc, th * 1024, 512),
                                          start=st_, stop=sp_)
                                ig = te.matmul(base[:, 512:1024], lg,
                                               xnT_sl(dc, th * 1024 + 512,
                                                      512),
                                               start=st_, stop=sp_)
                            E[("m3g", idx)] = mark(ig, "pe")
                            for ev in rd[1:]:
                                wt(te, ev)
                            for dc in range(NDC):
                                lu = wuT_sl(b2, dc * 512 + hcb * 128, 128)
                                st_ = (dc == 0)
                                sp_ = (dc == NDC - 1)
                                te.matmul(base[:, 1024:1536], lu,
                                          xnT_sl(dc, th * 1024, 512),
                                          start=st_, stop=sp_)
                                ins = te.matmul(base[:, 1536:2048], lu,
                                                xnT_sl(dc, th * 1024 + 512,
                                                       512),
                                                start=st_, stop=sp_)
                            E[("m3", idx)] = mark(ins, "pe")

                    @block.scalar
                    def _(a, b=b, hc=hc):
                        for th in range(2):
                            idx = hc * 2 + th
                            pi = (idx + b) % 2
                            base = ps03 if pi == 0 else ps47
                            wt(a, E[("m3g", idx)])
                            wt(a, E.get(("hact", idx - 2)))
                            E[("silu", idx)] = mark(a.activation(
                                scr32[:, th * 1024:(th + 1) * 1024],
                                base[:, 0:1024], AF.Silu, scale=S127), "ac")

                    for th in range(2):
                        pi = (hc * 2 + th + b) % 2
                        base = ps03 if pi == 0 else ps47
                        idx = hc * 2 + th

                        @block.vector
                        def _(v, b=b, hcb=hcb, th=th, idx=idx, base=base):
                            wt(v, E[("silu", idx)])
                            wt(v, E[("m3", idx)])
                            if b == 0 and hcb == 0 and th == 0:
                                wt(v, E[("attev", 15)])
                            E[("hact", idx)] = mark(v.tensor_tensor(
                                out=hact_sl(hcb * 2048 + th * 1024, 1024),
                                in0=scr32[:, th * 1024:(th + 1) * 1024],
                                in1=base[:, 1024:2048],
                                op=ALU.mult), "dv")

                        grp_readers[pi] = [E[("silu", idx)],
                                           E[("hact", idx)]]

                E[("M3done", b)] = E[("m3", (HBLK * b + HBLK - 1) * 2 + 1)]

                for tt in range(NT):
                    q5 = tt % 2

                    @block.tensor
                    def _(te, b=b, b2=b2, tt=tt, q5=q5,
                          last_g0_pass=last_g0_pass):
                        if tt == 0:
                            wt(te, E[("Wd", b)])
                            wt(te, E[("silu", last_g0_pass)])
                        if tt == 1:
                            wt(te, E[("hact", last_g0_pass)])
                        wt(te, E.get(("sttev", b, tt - 2)))
                        for hcb in range(HBLK):
                            if tt == 0:
                                wt(te, E[("hact", (HBLK * b + hcb) * 2 + 1)])
                            lh = hact_sl(hcb * 2048 + tt * 128, 128)
                            st_ = (hcb == 0)
                            sp_ = (hcb == HBLK - 1)
                            te.matmul(ps03[:, q5 * 1024: q5 * 1024 + 512],
                                      lh, wdT_sl(b2, hcb * 1024, 512),
                                      start=st_, stop=sp_)
                            ins = te.matmul(
                                ps03[:, q5 * 1024 + 512: q5 * 1024 + 1024],
                                lh, wdT_sl(b2, hcb * 1024 + 512, 512),
                                start=st_, stop=sp_)
                        E[("m5", b, tt)] = mark(ins, "pe")

                    @block.vector
                    def _(v, b=b, tt=tt, q5=q5):
                        wt(v, E[("m5", b, tt)])
                        E[("sttev", b, tt)] = mark(v.scalar_tensor_tensor(
                            out=xs(tt),
                            in0=ps03[:, q5 * 1024:(q5 + 1) * 1024],
                            scalar=S127SQ, in1=xs(tt),
                            op0=ALU.mult, op1=ALU.add), "dv")
                E[("M5done", b)] = E[("m5", b, NT - 1)]
                grp_readers[0] = [E[("sttev", b, NT - 2)],
                                  E[("sttev", b, NT - 1)]]

                if b + 2 < NBLK:
                    @block.sync
                    def _(sy, b=b):
                        bn = b + 2
                        b2n = bn % 2
                        h0n = bn * 512
                        sg_ = "g%d" % b2n
                        sd_ = "d%d" % b2n
                        wt(sy, E[("M3done", b)])
                        for dc in range(NDC):
                            mark(sy.dma_start(
                                out=wgT_sl(b2n, dc * 512, 512),
                                in_=wg_bf[h0n:h0n + 512,
                                          dc * 128:(dc + 1) * 128],
                                transpose=True), sg_, 16)
                            mark(sy.dma_start(
                                out=wuT_sl(b2n, dc * 512, 512),
                                in_=wu_bf[h0n:h0n + 512,
                                          dc * 128:(dc + 1) * 128],
                                transpose=True), sg_, 16)
                        E[("Wgu", bn)] = now(sg_)
                        wt(sy, E[("M5done", b)])
                        for hcb in range(HBLK):
                            mark(sy.dma_start(
                                out=wdT_sl(b2n, hcb * 1024, 1024),
                                in_=wd_bf[:, h0n + hcb * 128:
                                          h0n + (hcb + 1) * 128],
                                transpose=True), sd_, 16)
                        E[("Wd", bn)] = now(sd_)

            @block.sync
            def _(sy):
                for tt in range(NT):
                    wt(sy, E[("sttev", NBLK - 1, tt)])
                    mark(sy.dma_start(out=o_r[:, tt, :], in_=xs(tt)),
                         "st", 16)
                sy.wait_ge(sems["st"], cnt["st"])

    return nc


_IDENT = np.eye(128, dtype=ml_dtypes.bfloat16)


def kernel(L0_attn_x, L0_ffn_w_gate, L0_ffn_w_up, L0_ffn_w_down):
    x = np.ascontiguousarray(np.asarray(L0_attn_x, dtype=np.float32))
    wg = np.ascontiguousarray(np.asarray(L0_ffn_w_gate, dtype=np.int32))
    wu = np.ascontiguousarray(np.asarray(L0_ffn_w_up, dtype=np.int32))
    wd = np.ascontiguousarray(np.asarray(L0_ffn_w_down, dtype=np.int32))

    nc = build_nc()
    in_maps = [{"x": x[c], "wg": wg, "wu": wu, "wd": wd, "ident": _IDENT}
               for c in range(8)]
    res = run_bass_kernel_spmd(nc, in_maps, core_ids=list(range(8)))
    return np.stack([np.asarray(r["o"]) for r in res.results], axis=0)


# version 41 (final)
# speedup vs baseline: 1.3027x; 1.0022x over previous
import contextlib

import numpy as np
import ml_dtypes
import concourse.bass as bass
import concourse.mybir as mybir
from concourse.bass_utils import run_bass_kernel_spmd

F32 = mybir.dt.float32
BF16 = mybir.dt.bfloat16
I32 = mybir.dt.int32
AF = mybir.ActivationFunctionType
ALU = mybir.AluOpType
AX = mybir.AxisListType

T = 2048
D = 1024
H = 3072
NT = T // 128
NDC = D // 128
NHC = H // 128
HBLK = 4
NBLK = NHC // HBLK
S127 = 1.0 / 127.0
S127SQ = (1.0 / 127.0) ** 2
EPS = 1e-5


def build_nc():
    nc = bass.Bass("TRN2", target_bir_lowering=False)

    x_hbm = nc.dram_tensor("x", [T, D], F32, kind="ExternalInput").ap()
    wg_hbm = nc.dram_tensor("wg", [H, D], I32, kind="ExternalInput").ap()
    wu_hbm = nc.dram_tensor("wu", [H, D], I32, kind="ExternalInput").ap()
    wd_hbm = nc.dram_tensor("wd", [D, H], I32, kind="ExternalInput").ap()
    id_hbm = nc.dram_tensor("ident", [128, 128], BF16, kind="ExternalInput").ap()
    o_hbm = nc.dram_tensor("o", [T, D], F32, kind="ExternalOutput").ap()

    wg_bf = nc.dram_tensor("wg_bf", [H, D], BF16, kind="Internal").ap()
    wu_bf = nc.dram_tensor("wu_bf", [H, D], BF16, kind="Internal").ap()
    wd_bf = nc.dram_tensor("wd_bf", [D, H], BF16, kind="Internal").ap()

    x_r = x_hbm.rearrange("(t p) d -> p t d", p=128)
    o_r = o_hbm.rearrange("(t p) d -> p t d", p=128)

    with contextlib.ExitStack() as ctx:
        def sb(name, shape, dtype):
            return ctx.enter_context(nc.sbuf_tensor(name, shape, dtype))

        x_sb = sb("x_sb", [128, NT * D], F32)
        xnT = sb("xnT", [128, NDC * T], BF16)
        arena = sb("arena", [128, 40960], BF16)
        scr32 = sb("scr32", [128, 4096], F32)
        ptile = sb("ptile", [128, 2 * D], BF16)
        xnb = sb("xnb", [128, 4 * D], BF16)
        sb_id = sb("sb_id", [128, 128], BF16)
        st_eps = sb("st_eps", [128, 1], F32)
        st_bn = sb("st_bn", [128, 24], F32)
        st_mv2 = sb("st_mv2", [128, 2 * NT], F32)
        st_lnv = sb("st_lnv", [128, NT], F32)
        st_rstd = sb("st_rstd", [128, NT], F32)
        st_s = sb("st_s", [128, NT], F32)
        st_rs = sb("st_rs", [128, 2], F32)
        rs_bf = sb("rs_bf", [128, NT], BF16)
        recd = sb("recd", [128, 2], F32)
        s_lx = ctx.enter_context(nc.semaphore("lx"))
        s_ly = ctx.enter_context(nc.semaphore("ly"))
        s_lz = ctx.enter_context(nc.semaphore("lz"))
        s_lv = ctx.enter_context(nc.semaphore("lv"))
        s_l4 = ctx.enter_context(nc.semaphore("l4"))
        s_l5 = ctx.enter_context(nc.semaphore("l5"))
        s_l6 = ctx.enter_context(nc.semaphore("l6"))
        s_l7 = ctx.enter_context(nc.semaphore("l7"))
        s_li = ctx.enter_context(nc.semaphore("li"))
        s_g0 = ctx.enter_context(nc.semaphore("g0"))
        s_g1 = ctx.enter_context(nc.semaphore("g1"))
        s_d0 = ctx.enter_context(nc.semaphore("d0"))
        s_d1 = ctx.enter_context(nc.semaphore("d1"))
        s_ca = ctx.enter_context(nc.semaphore("ca"))
        s_st = ctx.enter_context(nc.semaphore("st"))
        s_pe = ctx.enter_context(nc.semaphore("pe"))
        s_ac = ctx.enter_context(nc.semaphore("ac"))
        s_dv = ctx.enter_context(nc.semaphore("dv"))
        block = ctx.enter_context(nc.Block())
        sems = {"lx": s_lx, "ly": s_ly, "lz": s_lz, "lv": s_lv,
                "l4": s_l4, "l5": s_l5, "l6": s_l6, "l7": s_l7,
                "li": s_li, "g0": s_g0, "g1": s_g1,
                "d0": s_d0, "d1": s_d1, "ca": s_ca, "st": s_st,
                "pe": s_pe, "ac": s_ac, "dv": s_dv}
        cnt = {k: 0 for k in sems}

        def mark(instr, sem, k=1):
            instr.then_inc(sems[sem], k)
            cnt[sem] += k
            return (sem, cnt[sem])

        def wt(eng, ev):
            if ev is not None:
                eng.wait_ge(sems[ev[0]], ev[1])

        def now(sem):
            return (sem, cnt[sem])

        PT0, SC0, XT0 = 0, 16384, 24576

        def pT_sl(dc, i0, n):
            return arena[:, PT0 + dc * T + i0: PT0 + dc * T + i0 + n]

        def sc_sl(jc, i0, n):
            return arena[:, SC0 + jc * 512 + i0: SC0 + jc * 512 + i0 + n]

        def xntok_sl(t, d0, n):
            return arena[:, XT0 + t * D + d0: XT0 + t * D + d0 + n]

        def wgT_sl(b2, c0, n):
            return arena[:, b2 * 4096 + c0: b2 * 4096 + c0 + n]

        def wuT_sl(b2, c0, n):
            return arena[:, 8192 + b2 * 4096 + c0: 8192 + b2 * 4096 + c0 + n]

        def wdT_sl(b2, c0, n):
            return arena[:, 16384 + b2 * 4096 + c0: 16384 + b2 * 4096 + c0 + n]

        def hact_sl(c0, n):
            return arena[:, XT0 + c0: XT0 + c0 + n]

        def xs(t):
            return x_sb[:, t * D:(t + 1) * D]

        def xnT_sl(dc, t0, n):
            return xnT[:, dc * T + t0: dc * T + t0 + n]

        E = {}

        @block.gpsimd
        def _(g):
            mark(g.dma_start(out=sb_id[:, :], in_=id_hbm[:, :]), "li", 16)
            for qi, sm in enumerate(("lx", "ly", "lz", "lv",
                                     "l4", "l5", "l6", "l7")):
                mark(g.dma_start(
                    out=x_sb[:, qi * 2 * D:(qi + 1) * 2 * D].rearrange(
                        "p (t d) -> p t d", t=2),
                    in_=x_r[:, qi * 2:(qi + 1) * 2, :]), sm, 16)
            for r in range(0, H, 768):
                mark(g.dma_start(out=wg_bf[r:r + 768, :],
                                 in_=wg_hbm[r:r + 768, :]), "ca", 16)
                mark(g.dma_start(out=wu_bf[r:r + 768, :],
                                 in_=wu_hbm[r:r + 768, :]), "ca", 16)
            for r in range(0, D, 256):
                mark(g.dma_start(out=wd_bf[r:r + 256, :],
                                 in_=wd_hbm[r:r + 256, :]), "ca", 16)
        E["ident"] = ("li", 16)
        for qi, sm in enumerate(("lx", "ly", "lz", "lv",
                                 "l4", "l5", "l6", "l7")):
            E[("xq8", qi)] = (sm, 16)

        @block.vector
        def _(v):
            mark(v.memset(st_eps[:, :], EPS), "dv")
        E["consts"] = now("dv")

        ps47 = ctx.enter_context(
            nc.psum_tensor("ps47", [128, 2048], F32, side="right"))

        last_pair_evac = {}

        def emit_m1_pair_te(sg, p):
            jcs = (2 * p, 2 * p + 1)
            reg = ps47[:, (p % 2) * 1024: (p % 2) * 1024 + 1024]

            @block.tensor
            def _(te, sg=sg, p=p, jcs=jcs, reg=reg):
                if p < 2:
                    wt(te, E[("pTe", sg * 4 + 3)])
                wt(te, E[("xnTe", 2 * p + 1)])
                wt(te, last_pair_evac.get(p % 2))
                if sg > 0 and p == 0:
                    wt(te, E[("recq", (sg - 1) * 4 + 3)])
                i0 = sg * 512
                for dc in range(NDC):
                    for k, jc in enumerate(jcs):
                        ins = te.matmul(
                            reg[:, k * 512:(k + 1) * 512],
                            xnT_sl(dc, jc * 128, 128),
                            pT_sl(dc, i0, 512),
                            start=(dc == 0), stop=(dc == NDC - 1))
                E[("scp", sg, p)] = mark(ins, "pe")

        def emit_m1_evac(sg, plist, first=False):
            @block.scalar
            def _(a, sg=sg, plist=plist, first=first):
                if first and sg > 0:
                    wt(a, E[("M2last", sg - 1)])
                for p in plist:
                    wt(a, E[("scp", sg, p)])
                    reg = ps47[:, (p % 2) * 1024: (p % 2) * 1024 + 1024]
                    for k, jc in enumerate((2 * p, 2 * p + 1)):
                        ins = a.activation(
                            sc_sl(jc, 0, 512), reg[:, k * 512:(k + 1) * 512],
                            AF.Exp)
                    E[("scpe", sg, p)] = mark(ins, "ac")
                    last_pair_evac[p % 2] = E[("scpe", sg, p)]

        def emit_m1_pair(sg, p):
            emit_m1_pair_te(sg, p)
            emit_m1_evac(sg, [p], first=(p == 0))

        with nc.psum_tensor("ps_bf", [128, 4096], BF16) as ps_bf:
            def emit_pass1(t):
                @block.vector
                def _(v, t=t):
                    qb = (t % 2) * 12
                    wt(v, E[("xq8", t // 2)])
                    wt(v, E.get(("mv", t - 2)))
                    mark(v.bn_stats(st_bn[:, qb:qb + 6], xs(t)[:, 0:512]),
                         "dv")
                    e1 = mark(v.bn_stats(st_bn[:, qb + 6:qb + 12],
                                         xs(t)[:, 512:1024]), "dv")
                    wt(v, e1)
                    E[("mv", t)] = mark(
                        v.bn_aggr(st_mv2[:, 2 * t: 2 * t + 2],
                                  st_bn[:, qb:qb + 12]), "dv")

            var_ap3 = st_mv2[:, :].rearrange("p (t two) -> p t two", two=2)
            def emit_pass2(pair):
                lo = pair * 2
                @block.scalar
                def _(a, lo=lo):
                    wt(a, E[("mv", lo + 1)])
                    e2 = mark(a.activation(
                        st_lnv[:, lo:lo + 2],
                        var_ap3[:, lo:lo + 2, 1:2], AF.Ln,
                        bias=st_eps[:, :]), "ac")
                    wt(a, e2)
                    E[("rstd", lo)] = mark(a.activation(
                        st_rstd[:, lo:lo + 2], st_lnv[:, lo:lo + 2],
                        AF.Exp, scale=-0.5), "ac")

            for t in range(8):
                emit_pass1(t)
            for pr in range(4):
                emit_pass2(pr)

            def emit_xnTe(eng, t):
                qp = t % 2
                dst = xnT[:, :].rearrange("p (dc t) -> p dc t", dc=NDC)[
                    :, :, t * 128:(t + 1) * 128]
                srcx = ps_bf[:, qp * 1024:(qp + 1) * 1024].rearrange(
                    "p (dc t) -> p dc t", dc=NDC)
                if hasattr(eng, "activation"):
                    E[("xnTe", t)] = mark(
                        eng.activation(dst, srcx, AF.Copy), "ac")
                else:
                    E[("xnTe", t)] = mark(
                        eng.tensor_copy(out=dst, in_=srcx), "dv")

            def emit_dve_pre(t):
                @block.vector
                def _(v, t=t):
                    q = t % 2
                    tq = t % 4
                    if t % 2 == 0:
                        wt(v, E[("rstd", t)])
                    xn32 = scr32[:, tq * D:(tq + 1) * D]
                    e3 = mark(v.tensor_scalar(
                        out=xn32, in0=xs(t),
                        scalar1=st_mv2[:, 2 * t: 2 * t + 1],
                        scalar2=st_rstd[:, t:t + 1],
                        op0=ALU.subtract, op1=ALU.mult), "dv")
                    wt(v, e3)
                    wt(v, E.get(("xnTr", t - 4)))
                    E[("xncast", t)] = mark(
                        v.tensor_copy(out=xnb[:, tq * D:(tq + 1) * D],
                                      in_=xn32), "dv")

            def emit_dve_post(t, with_pte=True):
                @block.vector
                def _(v, t=t, with_pte=with_pte):
                    q = t % 2
                    tq = t % 4
                    wt(v, E[("exp", t)])
                    e5 = mark(v.reciprocal(st_rs[:, q:q + 1],
                                           st_s[:, t:t + 1]), "dv")
                    wt(v, e5)
                    wt(v, E.get(("ptr", t - 2)))
                    e32 = scr32[:, tq * D:(tq + 1) * D]
                    E[("pcast", t)] = mark(v.tensor_scalar(
                        out=ptile[:, q * D:(q + 1) * D], in0=e32,
                        scalar1=st_rs[:, q:q + 1], scalar2=None,
                        op0=ALU.mult), "dv")
                    mark(v.tensor_copy(out=rs_bf[:, t:t + 1],
                                       in_=st_rs[:, q:q + 1]), "dv")
                    mark(v.tensor_scalar(
                        out=xntok_sl(t, 0, D),
                        in0=xnb[:, tq * D:(tq + 1) * D],
                        scalar1=st_rs[:, q:q + 1], scalar2=None,
                        op0=ALU.mult), "dv")
                    if t >= 1 and with_pte:
                        qp = (t - 1) % 2
                        wt(v, E[("ptr", t - 1)])
                        dstp = arena[:, PT0:PT0 + NDC * T].rearrange(
                            "p (dc t) -> p dc t", dc=NDC)[
                            :, :, (t - 1) * 128: t * 128]
                        srcp = ps_bf[:, 2048 + qp * 1024:
                                     2048 + (qp + 1) * 1024].rearrange(
                            "p (dc t) -> p dc t", dc=NDC)
                        E[("pTe", t - 1)] = mark(
                            v.tensor_copy(out=dstp, in_=srcp), "dv")

            def emit_pe_transp_xn(t):
                @block.tensor
                def _(te, t=t):
                    q = t % 2
                    tq = t % 4
                    wt(te, E["ident"])
                    wt(te, E[("xncast", t)])
                    wt(te, E.get(("xnTe", t - 2)))
                    for dc in range(NDC):
                        ins = te.transpose(
                            ps_bf[:, q * 1024 + dc * 128:
                                  q * 1024 + dc * 128 + 128],
                            xnb[:, tq * D + dc * 128: tq * D + dc * 128 + 128],
                            sb_id[:, :])
                    E[("xnTr", t)] = mark(ins, "pe")

            def emit_pe_transp_p(t):
                @block.tensor
                def _(te, t=t):
                    q = t % 2
                    wt(te, E[("pcast", t)])
                    wt(te, E.get(("pTe", t - 2)))
                    for dc in range(NDC):
                        ins = te.transpose(
                            ps_bf[:, 2048 + q * 1024 + dc * 128:
                                  2048 + q * 1024 + dc * 128 + 128],
                            ptile[:, q * D + dc * 128: q * D + dc * 128 + 128],
                            sb_id[:, :])
                    E[("ptr", t)] = mark(ins, "pe")

            def emit_pe_transp(t):
                emit_pe_transp_xn(t)
                emit_pe_transp_p(t)

            emit_dve_pre(0)
            emit_dve_pre(1)
            for k in range(NT // 2):
                t0, t1 = 2 * k, 2 * k + 1

                @block.scalar
                def _(a, k=k, t0=t0, t1=t1):
                    wt(a, E[("xncast", t0)])
                    wt(a, E[("xncast", t1)])
                    wt(a, E.get(("pcast", t0 - 4)))
                    wt(a, E.get(("pcast", t1 - 4)))
                    if k >= 1:
                        wt(a, E[("xnTr", t0 - 2)])
                        wt(a, E[("xnTr", t0 - 1)])
                    if k < 6:
                        for t in (t0, t1):
                            tq = t % 4
                            xn32 = scr32[:, tq * D:(tq + 1) * D]
                            E[("exp", t)] = mark(
                                a.activation(xn32, xn32, AF.Exp,
                                             accum_out=st_s[:, t:t + 1]),
                                "ac")
                    if k >= 1:
                        emit_xnTe(a, t0 - 2)
                        emit_xnTe(a, t0 - 1)

                if k < NT // 2 - 1:
                    if k < 4:
                        emit_pass1(8 + 2 * k)
                        emit_pass1(9 + 2 * k)
                        if k == 1:
                            emit_pass2(4)
                            emit_pass2(5)
                        if k == 3:
                            emit_pass2(6)
                            emit_pass2(7)
                    emit_dve_pre(t0 + 2)
                    emit_dve_pre(t1 + 2)
                if k < 6:
                    emit_dve_post(t0)
                    emit_pe_transp(t0)
                    emit_dve_post(t1)
                    emit_pe_transp(t1)
                else:
                    emit_pe_transp_xn(t0)
                    emit_pe_transp_xn(t1)

            @block.scalar
            def _(a):
                wt(a, E[("xnTr", NT - 2)])
                wt(a, E[("xnTr", NT - 1)])
                emit_xnTe(a, NT - 2)
                emit_xnTe(a, NT - 1)


        with nc.psum_tensor("ps03", [128, 2048], F32) as ps03:

            def emit_m2(sg):
                for isub in range(4):
                    ii = sg * 4 + isub
                    q2 = isub % 2

                    @block.tensor
                    def _(te, sg=sg, isub=isub, ii=ii, q2=q2):
                        attA = ps03[:, q2 * 1024: q2 * 1024 + 512]
                        attB = ps03[:, q2 * 1024 + 512: q2 * 1024 + 1024]
                        den = ps47[:, q2 * 512: q2 * 512 + 1]
                        wt(te, E.get(("attev", ii - 2)))
                        wt(te, E[("scpe", sg, 6)])
                        if sg == 0 and isub < 2:
                            wt(te, E[("xnTe", NT - 1)])
                            wt(te, E["consts"])
                            if isub == 0:
                                wt(te, E[("pTe", 13)])
                            else:
                                wt(te, E[("pTe", NT - 1)])
                        for jc in range(NT):
                            if jc == 14:
                                wt(te, E[("scpe", sg, 7)])
                                if sg == 0 and isub == 0:
                                    wt(te, E[("pTe", NT - 1)])
                            lhsT = sc_sl(jc, isub * 128, 128)
                            st_ = (jc == 0)
                            sp_ = (jc == NT - 1)
                            te.matmul(attA, lhsT, xntok_sl(jc, 0, 512),
                                      start=st_, stop=sp_)
                            te.matmul(attB, lhsT, xntok_sl(jc, 512, 512),
                                      start=st_, stop=sp_)
                            ins = te.matmul(den, lhsT,
                                            rs_bf[:, jc:jc + 1],
                                            start=st_, stop=sp_)
                        E[("att", ii)] = mark(ins, "pe")

                    @block.vector
                    def _(v, ii=ii, q2=q2):
                        wt(v, E[("att", ii)])
                        den = ps47[:, q2 * 512: q2 * 512 + 1]
                        E[("recq", ii)] = e5 = mark(
                            v.reciprocal(recd[:, q2:q2 + 1], den), "dv")
                        wt(v, e5)
                        E[("attev", ii)] = mark(v.scalar_tensor_tensor(
                            out=xs(ii), in0=ps03[:, q2 * 1024:
                                                 (q2 + 1) * 1024],
                            scalar=recd[:, q2:q2 + 1], in1=xs(ii),
                            op0=ALU.mult, op1=ALU.add), "dv")
                E[("M2last", sg)] = E[("att", sg * 4 + 3)]

            @block.scalar
            def _(a):
                for t in range(12, 16):
                    wt(a, E[("xncast", t)])
                    wt(a, E[("pcast", t - 4)])
                for t in range(12, 16):
                    tq = t % 4
                    xn32 = scr32[:, tq * D:(tq + 1) * D]
                    E[("exp", t)] = mark(
                        a.activation(xn32, xn32, AF.Exp,
                                     accum_out=st_s[:, t:t + 1]), "ac")

            emit_dve_post(12, with_pte=True)
            emit_dve_post(13, with_pte=False)

            for p in range(4):
                emit_m1_pair_te(0, p)
                if p % 2 == 1:
                    emit_m1_evac(0, [p - 1, p])

            emit_pe_transp_p(12)
            emit_pe_transp_p(13)

            @block.vector
            def _(v):
                for t in (12, 13):
                    qp = t % 2
                    wt(v, E[("ptr", t)])
                    dstp = arena[:, PT0:PT0 + NDC * T].rearrange(
                        "p (dc t) -> p dc t", dc=NDC)[
                        :, :, t * 128:(t + 1) * 128]
                    srcp = ps_bf[:, 2048 + qp * 1024: 2048 + (qp + 1) * 1024
                                 ].rearrange("p (dc t) -> p dc t", dc=NDC)
                    E[("pTe", t)] = mark(
                        v.tensor_copy(out=dstp, in_=srcp), "dv")

            emit_dve_post(14, with_pte=False)
            emit_dve_post(15, with_pte=False)

            for p in range(4, 8):
                emit_m1_pair_te(0, p)
                if p % 2 == 1:
                    emit_m1_evac(0, [p - 1, p])
            E[("M1last", 0)] = E[("scp", 0, 7)]

            emit_pe_transp_p(14)
            emit_pe_transp_p(15)

            @block.vector
            def _(v):
                for t in (14, 15):
                    qp = t % 2
                    wt(v, E[("ptr", t)])
                    dstp = arena[:, PT0:PT0 + NDC * T].rearrange(
                        "p (dc t) -> p dc t", dc=NDC)[
                        :, :, t * 128:(t + 1) * 128]
                    srcp = ps_bf[:, 2048 + qp * 1024: 2048 + (qp + 1) * 1024
                                 ].rearrange("p (dc t) -> p dc t", dc=NDC)
                    E[("pTe", t)] = mark(
                        v.tensor_copy(out=dstp, in_=srcp), "dv")

            emit_m2(0)
            for sg in range(1, 4):
                for p in range(8):
                    emit_m1_pair_te(sg, p)
                    if p % 2 == 1:
                        emit_m1_evac(sg, [p - 1, p], first=(p == 1))
                E[("M1last", sg)] = E[("scp", sg, 7)]
                emit_m2(sg)

            grp_readers = {0: [E[("attev", 14)], E[("attev", 15)]],
                           1: [E[("scpe", 3, 7)], E[("attev", 15)]]}

            @block.sync
            def _(sy):
                sy.wait_ge(sems["ca"], cnt["ca"])
                for b in (0, 1):
                    b2 = b % 2
                    h0 = b * 512
                    sg_ = "g%d" % b2
                    sd_ = "d%d" % b2
                    wt(sy, E[("M1last", 3)])
                    for dc in range(NDC):
                        mark(sy.dma_start(
                            out=wgT_sl(b2, dc * 512, 512),
                            in_=wg_bf[h0:h0 + 512, dc * 128:(dc + 1) * 128],
                            transpose=True), sg_, 16)
                        mark(sy.dma_start(
                            out=wuT_sl(b2, dc * 512, 512),
                            in_=wu_bf[h0:h0 + 512, dc * 128:(dc + 1) * 128],
                            transpose=True), sg_, 16)
                    E[("Wgu", b)] = now(sg_)
                    wt(sy, E[("M2last", 3)])
                    for hcb in range(HBLK):
                        mark(sy.dma_start(
                            out=wdT_sl(b2, hcb * 1024, 1024),
                            in_=wd_bf[:, h0 + hcb * 128: h0 + (hcb + 1) * 128],
                            transpose=True), sd_, 16)
                    E[("Wd", b)] = now(sd_)

            for b in range(NBLK):
                b2 = b % 2
                h0 = b * 512
                last_g0_pass = None

                for hcb in range(HBLK):
                    hc = HBLK * b + hcb
                    for th in range(2):
                        pi = (hc * 2 + th + b) % 2
                        base = ps03 if pi == 0 else ps47
                        idx = hc * 2 + th
                        if pi == 0:
                            last_g0_pass = idx

                        @block.tensor
                        def _(te, b=b, b2=b2, hcb=hcb, th=th, pi=pi,
                              base=base, idx=idx):
                            if hcb == 0 and th == 0:
                                wt(te, E[("Wgu", b)])
                            rd = grp_readers[pi]
                            wt(te, rd[0])
                            for dc in range(NDC):
                                lg = wgT_sl(b2, dc * 512 + hcb * 128, 128)
                                st_ = (dc == 0)
                                sp_ = (dc == NDC - 1)
                                te.matmul(base[:, 0:512], lg,
                                          xnT_sl(dc, th * 1024, 512),
                                          start=st_, stop=sp_)
                                ig = te.matmul(base[:, 512:1024], lg,
                                               xnT_sl(dc, th * 1024 + 512,
                                                      512),
                                               start=st_, stop=sp_)
                            E[("m3g", idx)] = mark(ig, "pe")
                            for ev in rd[1:]:
                                wt(te, ev)
                            for dc in range(NDC):
                                lu = wuT_sl(b2, dc * 512 + hcb * 128, 128)
                                st_ = (dc == 0)
                                sp_ = (dc == NDC - 1)
                                te.matmul(base[:, 1024:1536], lu,
                                          xnT_sl(dc, th * 1024, 512),
                                          start=st_, stop=sp_)
                                ins = te.matmul(base[:, 1536:2048], lu,
                                                xnT_sl(dc, th * 1024 + 512,
                                                       512),
                                                start=st_, stop=sp_)
                            E[("m3", idx)] = mark(ins, "pe")

                    @block.scalar
                    def _(a, b=b, hc=hc):
                        for th in range(2):
                            idx = hc * 2 + th
                            pi = (idx + b) % 2
                            base = ps03 if pi == 0 else ps47
                            wt(a, E[("m3g", idx)])
                            wt(a, E.get(("hact", idx - 2)))
                            E[("silu", idx)] = mark(a.activation(
                                scr32[:, th * 1024:(th + 1) * 1024],
                                base[:, 0:1024], AF.Silu, scale=S127), "ac")

                    for th in range(2):
                        pi = (hc * 2 + th + b) % 2
                        base = ps03 if pi == 0 else ps47
                        idx = hc * 2 + th

                        @block.vector
                        def _(v, b=b, hcb=hcb, th=th, idx=idx, base=base):
                            wt(v, E[("silu", idx)])
                            wt(v, E[("m3", idx)])
                            if b == 0 and hcb == 0 and th == 0:
                                wt(v, E[("attev", 15)])
                            E[("hact", idx)] = mark(v.tensor_tensor(
                                out=hact_sl(hcb * 2048 + th * 1024, 1024),
                                in0=scr32[:, th * 1024:(th + 1) * 1024],
                                in1=base[:, 1024:2048],
                                op=ALU.mult), "dv")

                        grp_readers[pi] = [E[("silu", idx)],
                                           E[("hact", idx)]]

                E[("M3done", b)] = E[("m3", (HBLK * b + HBLK - 1) * 2 + 1)]

                for tt in range(NT):
                    q5 = tt % 2

                    @block.tensor
                    def _(te, b=b, b2=b2, tt=tt, q5=q5,
                          last_g0_pass=last_g0_pass):
                        if tt == 0:
                            wt(te, E[("Wd", b)])
                            wt(te, E[("silu", last_g0_pass)])
                        if tt == 1:
                            wt(te, E[("hact", last_g0_pass)])
                        wt(te, E.get(("sttev", b, tt - 2)))
                        for hcb in range(HBLK):
                            if tt == 0:
                                wt(te, E[("hact", (HBLK * b + hcb) * 2 + 1)])
                            lh = hact_sl(hcb * 2048 + tt * 128, 128)
                            st_ = (hcb == 0)
                            sp_ = (hcb == HBLK - 1)
                            te.matmul(ps03[:, q5 * 1024: q5 * 1024 + 512],
                                      lh, wdT_sl(b2, hcb * 1024, 512),
                                      start=st_, stop=sp_)
                            ins = te.matmul(
                                ps03[:, q5 * 1024 + 512: q5 * 1024 + 1024],
                                lh, wdT_sl(b2, hcb * 1024 + 512, 512),
                                start=st_, stop=sp_)
                        E[("m5", b, tt)] = mark(ins, "pe")

                    @block.vector
                    def _(v, b=b, tt=tt, q5=q5):
                        wt(v, E[("m5", b, tt)])
                        E[("sttev", b, tt)] = mark(v.scalar_tensor_tensor(
                            out=xs(tt),
                            in0=ps03[:, q5 * 1024:(q5 + 1) * 1024],
                            scalar=S127SQ, in1=xs(tt),
                            op0=ALU.mult, op1=ALU.add), "dv")
                E[("M5done", b)] = E[("m5", b, NT - 1)]
                grp_readers[0] = [E[("sttev", b, NT - 2)],
                                  E[("sttev", b, NT - 1)]]

                if b + 2 < NBLK:
                    @block.sync
                    def _(sy, b=b):
                        bn = b + 2
                        b2n = bn % 2
                        h0n = bn * 512
                        sg_ = "g%d" % b2n
                        sd_ = "d%d" % b2n
                        wt(sy, E[("M3done", b)])
                        for dc in range(NDC):
                            mark(sy.dma_start(
                                out=wgT_sl(b2n, dc * 512, 512),
                                in_=wg_bf[h0n:h0n + 512,
                                          dc * 128:(dc + 1) * 128],
                                transpose=True), sg_, 16)
                            mark(sy.dma_start(
                                out=wuT_sl(b2n, dc * 512, 512),
                                in_=wu_bf[h0n:h0n + 512,
                                          dc * 128:(dc + 1) * 128],
                                transpose=True), sg_, 16)
                        E[("Wgu", bn)] = now(sg_)
                        wt(sy, E[("M5done", b)])
                        for hcb in range(HBLK):
                            mark(sy.dma_start(
                                out=wdT_sl(b2n, hcb * 1024, 1024),
                                in_=wd_bf[:, h0n + hcb * 128:
                                          h0n + (hcb + 1) * 128],
                                transpose=True), sd_, 16)
                        E[("Wd", bn)] = now(sd_)

            @block.sync
            def _(sy):
                for tt in range(NT):
                    wt(sy, E[("sttev", NBLK - 1, tt)])
                    mark(sy.dma_start(out=o_r[:, tt, :], in_=xs(tt)),
                         "st", 16)
                sy.wait_ge(sems["st"], cnt["st"])

    return nc


_IDENT = np.eye(128, dtype=ml_dtypes.bfloat16)


def kernel(L0_attn_x, L0_ffn_w_gate, L0_ffn_w_up, L0_ffn_w_down):
    x = np.ascontiguousarray(np.asarray(L0_attn_x, dtype=np.float32))
    wg = np.ascontiguousarray(np.asarray(L0_ffn_w_gate, dtype=np.int32))
    wu = np.ascontiguousarray(np.asarray(L0_ffn_w_up, dtype=np.int32))
    wd = np.ascontiguousarray(np.asarray(L0_ffn_w_down, dtype=np.int32))

    nc = build_nc()
    in_maps = [{"x": x[c], "wg": wg, "wu": wu, "wd": wd, "ident": _IDENT}
               for c in range(8)]
    res = run_bass_kernel_spmd(nc, in_maps, core_ids=list(range(8)))
    return np.stack([np.asarray(r["o"]) for r in res.results], axis=0)


# version 42
# speedup vs baseline: 1.3044x; 1.0013x over previous
import contextlib

import numpy as np
import ml_dtypes
import concourse.bass as bass
import concourse.mybir as mybir
from concourse.bass_utils import run_bass_kernel_spmd

F32 = mybir.dt.float32
BF16 = mybir.dt.bfloat16
I32 = mybir.dt.int32
AF = mybir.ActivationFunctionType
ALU = mybir.AluOpType
AX = mybir.AxisListType

T = 2048
D = 1024
H = 3072
NT = T // 128
NDC = D // 128
NHC = H // 128
HBLK = 4
NBLK = NHC // HBLK
S127 = 1.0 / 127.0
S127SQ = (1.0 / 127.0) ** 2
EPS = 1e-5


def build_nc():
    nc = bass.Bass("TRN2", target_bir_lowering=False)

    x_hbm = nc.dram_tensor("x", [T, D], F32, kind="ExternalInput").ap()
    wg_hbm = nc.dram_tensor("wg", [H, D], I32, kind="ExternalInput").ap()
    wu_hbm = nc.dram_tensor("wu", [H, D], I32, kind="ExternalInput").ap()
    wd_hbm = nc.dram_tensor("wd", [D, H], I32, kind="ExternalInput").ap()
    id_hbm = nc.dram_tensor("ident", [128, 128], BF16, kind="ExternalInput").ap()
    o_hbm = nc.dram_tensor("o", [T, D], F32, kind="ExternalOutput").ap()

    wg_bf = nc.dram_tensor("wg_bf", [H, D], BF16, kind="Internal").ap()
    wu_bf = nc.dram_tensor("wu_bf", [H, D], BF16, kind="Internal").ap()
    wd_bf = nc.dram_tensor("wd_bf", [D, H], BF16, kind="Internal").ap()

    x_r = x_hbm.rearrange("(t p) d -> p t d", p=128)
    o_r = o_hbm.rearrange("(t p) d -> p t d", p=128)

    with contextlib.ExitStack() as ctx:
        def sb(name, shape, dtype):
            return ctx.enter_context(nc.sbuf_tensor(name, shape, dtype))

        x_sb = sb("x_sb", [128, NT * D], F32)
        xnT = sb("xnT", [128, NDC * T], BF16)
        arena = sb("arena", [128, 40960], BF16)
        scr32 = sb("scr32", [128, 4096], F32)
        ptile = sb("ptile", [128, 2 * D], BF16)
        xnb = sb("xnb", [128, 4 * D], BF16)
        sb_id = sb("sb_id", [128, 128], BF16)
        st_eps = sb("st_eps", [128, 1], F32)
        st_bn = sb("st_bn", [128, 24], F32)
        st_mv2 = sb("st_mv2", [128, 2 * NT], F32)
        st_lnv = sb("st_lnv", [128, NT], F32)
        st_rstd = sb("st_rstd", [128, NT], F32)
        st_s = sb("st_s", [128, NT], F32)
        st_rs = sb("st_rs", [128, 2], F32)
        rs_bf = sb("rs_bf", [128, NT], BF16)
        recd = sb("recd", [128, 2], F32)
        s_lx = ctx.enter_context(nc.semaphore("lx"))
        s_ly = ctx.enter_context(nc.semaphore("ly"))
        s_lz = ctx.enter_context(nc.semaphore("lz"))
        s_lv = ctx.enter_context(nc.semaphore("lv"))
        s_l4 = ctx.enter_context(nc.semaphore("l4"))
        s_l5 = ctx.enter_context(nc.semaphore("l5"))
        s_l6 = ctx.enter_context(nc.semaphore("l6"))
        s_l7 = ctx.enter_context(nc.semaphore("l7"))
        s_li = ctx.enter_context(nc.semaphore("li"))
        s_g0 = ctx.enter_context(nc.semaphore("g0"))
        s_g1 = ctx.enter_context(nc.semaphore("g1"))
        s_d0 = ctx.enter_context(nc.semaphore("d0"))
        s_d1 = ctx.enter_context(nc.semaphore("d1"))
        s_ca = ctx.enter_context(nc.semaphore("ca"))
        s_st = ctx.enter_context(nc.semaphore("st"))
        s_pe = ctx.enter_context(nc.semaphore("pe"))
        s_ac = ctx.enter_context(nc.semaphore("ac"))
        s_dv = ctx.enter_context(nc.semaphore("dv"))
        block = ctx.enter_context(nc.Block())
        sems = {"lx": s_lx, "ly": s_ly, "lz": s_lz, "lv": s_lv,
                "l4": s_l4, "l5": s_l5, "l6": s_l6, "l7": s_l7,
                "li": s_li, "g0": s_g0, "g1": s_g1,
                "d0": s_d0, "d1": s_d1, "ca": s_ca, "st": s_st,
                "pe": s_pe, "ac": s_ac, "dv": s_dv}
        cnt = {k: 0 for k in sems}

        def mark(instr, sem, k=1):
            instr.then_inc(sems[sem], k)
            cnt[sem] += k
            return (sem, cnt[sem])

        def wt(eng, ev):
            if ev is not None:
                eng.wait_ge(sems[ev[0]], ev[1])

        def now(sem):
            return (sem, cnt[sem])

        PT0, SC0, XT0 = 0, 16384, 24576

        def pT_sl(dc, i0, n):
            return arena[:, PT0 + dc * T + i0: PT0 + dc * T + i0 + n]

        def sc_sl(jc, i0, n):
            return arena[:, SC0 + jc * 512 + i0: SC0 + jc * 512 + i0 + n]

        def xntok_sl(t, d0, n):
            return arena[:, XT0 + t * D + d0: XT0 + t * D + d0 + n]

        def wgT_sl(b2, c0, n):
            return arena[:, b2 * 4096 + c0: b2 * 4096 + c0 + n]

        def wuT_sl(b2, c0, n):
            return arena[:, 8192 + b2 * 4096 + c0: 8192 + b2 * 4096 + c0 + n]

        def wdT_sl(b2, c0, n):
            return arena[:, 16384 + b2 * 4096 + c0: 16384 + b2 * 4096 + c0 + n]

        def hact_sl(c0, n):
            return arena[:, XT0 + c0: XT0 + c0 + n]

        def xs(t):
            return x_sb[:, t * D:(t + 1) * D]

        def xnT_sl(dc, t0, n):
            return xnT[:, dc * T + t0: dc * T + t0 + n]

        E = {}

        @block.gpsimd
        def _(g):
            mark(g.dma_start(out=sb_id[:, :], in_=id_hbm[:, :]), "li", 16)
            for qi, sm in enumerate(("lx", "ly", "lz", "lv",
                                     "l4", "l5", "l6", "l7")):
                mark(g.dma_start(
                    out=x_sb[:, qi * 2 * D:(qi + 1) * 2 * D].rearrange(
                        "p (t d) -> p t d", t=2),
                    in_=x_r[:, qi * 2:(qi + 1) * 2, :]), sm, 16)
            for r in range(0, H, 768):
                mark(g.dma_start(out=wg_bf[r:r + 768, :],
                                 in_=wg_hbm[r:r + 768, :]), "ca", 16)
                mark(g.dma_start(out=wu_bf[r:r + 768, :],
                                 in_=wu_hbm[r:r + 768, :]), "ca", 16)
            for r in range(0, D, 256):
                mark(g.dma_start(out=wd_bf[r:r + 256, :],
                                 in_=wd_hbm[r:r + 256, :]), "ca", 16)
        E["ident"] = ("li", 16)
        for qi, sm in enumerate(("lx", "ly", "lz", "lv",
                                 "l4", "l5", "l6", "l7")):
            E[("xq8", qi)] = (sm, 16)

        @block.vector
        def _(v):
            mark(v.memset(st_eps[:, :], EPS), "dv")
        E["consts"] = now("dv")

        ps47 = ctx.enter_context(
            nc.psum_tensor("ps47", [128, 2048], F32, side="right"))

        last_pair_evac = {}

        def emit_m1_pair_te(sg, p):
            jcs = (2 * p, 2 * p + 1)
            reg = ps47[:, (p % 2) * 1024: (p % 2) * 1024 + 1024]

            @block.tensor
            def _(te, sg=sg, p=p, jcs=jcs, reg=reg):
                if p < 2:
                    wt(te, E[("pTe", sg * 4 + 3)])
                wt(te, E[("xnTe", 2 * p + 1)])
                wt(te, last_pair_evac.get(p % 2))
                if sg > 0 and p == 0:
                    wt(te, E[("recq", (sg - 1) * 4 + 2)])
                i0 = sg * 512
                for dc in range(NDC):
                    for k, jc in enumerate(jcs):
                        if sg > 0 and p == 0 and dc == 0 and k == 1:
                            wt(te, E[("recq", (sg - 1) * 4 + 3)])
                        ins = te.matmul(
                            reg[:, k * 512:(k + 1) * 512],
                            xnT_sl(dc, jc * 128, 128),
                            pT_sl(dc, i0, 512),
                            start=(dc == 0), stop=(dc == NDC - 1))
                E[("scp", sg, p)] = mark(ins, "pe")

        def emit_m1_evac(sg, plist, first=False):
            @block.scalar
            def _(a, sg=sg, plist=plist, first=first):
                if first and sg > 0:
                    wt(a, E[("M2last", sg - 1)])
                for p in plist:
                    wt(a, E[("scp", sg, p)])
                    reg = ps47[:, (p % 2) * 1024: (p % 2) * 1024 + 1024]
                    for k, jc in enumerate((2 * p, 2 * p + 1)):
                        ins = a.activation(
                            sc_sl(jc, 0, 512), reg[:, k * 512:(k + 1) * 512],
                            AF.Exp)
                    E[("scpe", sg, p)] = mark(ins, "ac")
                    last_pair_evac[p % 2] = E[("scpe", sg, p)]

        def emit_m1_pair(sg, p):
            emit_m1_pair_te(sg, p)
            emit_m1_evac(sg, [p], first=(p == 0))

        with nc.psum_tensor("ps_bf", [128, 4096], BF16) as ps_bf:
            def emit_pass1(t):
                @block.vector
                def _(v, t=t):
                    qb = (t % 2) * 12
                    wt(v, E[("xq8", t // 2)])
                    wt(v, E.get(("mv", t - 2)))
                    mark(v.bn_stats(st_bn[:, qb:qb + 6], xs(t)[:, 0:512]),
                         "dv")
                    e1 = mark(v.bn_stats(st_bn[:, qb + 6:qb + 12],
                                         xs(t)[:, 512:1024]), "dv")
                    wt(v, e1)
                    E[("mv", t)] = mark(
                        v.bn_aggr(st_mv2[:, 2 * t: 2 * t + 2],
                                  st_bn[:, qb:qb + 12]), "dv")

            var_ap3 = st_mv2[:, :].rearrange("p (t two) -> p t two", two=2)
            def emit_pass2(pair):
                lo = pair * 2
                @block.scalar
                def _(a, lo=lo):
                    wt(a, E[("mv", lo + 1)])
                    e2 = mark(a.activation(
                        st_lnv[:, lo:lo + 2],
                        var_ap3[:, lo:lo + 2, 1:2], AF.Ln,
                        bias=st_eps[:, :]), "ac")
                    wt(a, e2)
                    E[("rstd", lo)] = mark(a.activation(
                        st_rstd[:, lo:lo + 2], st_lnv[:, lo:lo + 2],
                        AF.Exp, scale=-0.5), "ac")

            for t in range(8):
                emit_pass1(t)
            for pr in range(4):
                emit_pass2(pr)

            def emit_xnTe(eng, t):
                qp = t % 2
                dst = xnT[:, :].rearrange("p (dc t) -> p dc t", dc=NDC)[
                    :, :, t * 128:(t + 1) * 128]
                srcx = ps_bf[:, qp * 1024:(qp + 1) * 1024].rearrange(
                    "p (dc t) -> p dc t", dc=NDC)
                if hasattr(eng, "activation"):
                    E[("xnTe", t)] = mark(
                        eng.activation(dst, srcx, AF.Copy), "ac")
                else:
                    E[("xnTe", t)] = mark(
                        eng.tensor_copy(out=dst, in_=srcx), "dv")

            def emit_dve_pre(t):
                @block.vector
                def _(v, t=t):
                    q = t % 2
                    tq = t % 4
                    if t % 2 == 0:
                        wt(v, E[("rstd", t)])
                    xn32 = scr32[:, tq * D:(tq + 1) * D]
                    e3 = mark(v.tensor_scalar(
                        out=xn32, in0=xs(t),
                        scalar1=st_mv2[:, 2 * t: 2 * t + 1],
                        scalar2=st_rstd[:, t:t + 1],
                        op0=ALU.subtract, op1=ALU.mult), "dv")
                    wt(v, e3)
                    wt(v, E.get(("xnTr", t - 4)))
                    E[("xncast", t)] = mark(
                        v.tensor_copy(out=xnb[:, tq * D:(tq + 1) * D],
                                      in_=xn32), "dv")

            def emit_dve_post(t, with_pte=True):
                @block.vector
                def _(v, t=t, with_pte=with_pte):
                    q = t % 2
                    tq = t % 4
                    wt(v, E[("exp", t)])
                    e5 = mark(v.reciprocal(st_rs[:, q:q + 1],
                                           st_s[:, t:t + 1]), "dv")
                    wt(v, e5)
                    wt(v, E.get(("ptr", t - 2)))
                    e32 = scr32[:, tq * D:(tq + 1) * D]
                    E[("pcast", t)] = mark(v.tensor_scalar(
                        out=ptile[:, q * D:(q + 1) * D], in0=e32,
                        scalar1=st_rs[:, q:q + 1], scalar2=None,
                        op0=ALU.mult), "dv")
                    mark(v.tensor_copy(out=rs_bf[:, t:t + 1],
                                       in_=st_rs[:, q:q + 1]), "dv")
                    mark(v.tensor_scalar(
                        out=xntok_sl(t, 0, D),
                        in0=xnb[:, tq * D:(tq + 1) * D],
                        scalar1=st_rs[:, q:q + 1], scalar2=None,
                        op0=ALU.mult), "dv")
                    if t >= 1 and with_pte:
                        qp = (t - 1) % 2
                        wt(v, E[("ptr", t - 1)])
                        dstp = arena[:, PT0:PT0 + NDC * T].rearrange(
                            "p (dc t) -> p dc t", dc=NDC)[
                            :, :, (t - 1) * 128: t * 128]
                        srcp = ps_bf[:, 2048 + qp * 1024:
                                     2048 + (qp + 1) * 1024].rearrange(
                            "p (dc t) -> p dc t", dc=NDC)
                        E[("pTe", t - 1)] = mark(
                            v.tensor_copy(out=dstp, in_=srcp), "dv")

            def emit_pe_transp_xn(t):
                @block.tensor
                def _(te, t=t):
                    q = t % 2
                    tq = t % 4
                    wt(te, E["ident"])
                    wt(te, E[("xncast", t)])
                    wt(te, E.get(("xnTe", t - 2)))
                    for dc in range(NDC):
                        ins = te.transpose(
                            ps_bf[:, q * 1024 + dc * 128:
                                  q * 1024 + dc * 128 + 128],
                            xnb[:, tq * D + dc * 128: tq * D + dc * 128 + 128],
                            sb_id[:, :])
                    E[("xnTr", t)] = mark(ins, "pe")

            def emit_pe_transp_p(t):
                @block.tensor
                def _(te, t=t):
                    q = t % 2
                    wt(te, E[("pcast", t)])
                    wt(te, E.get(("pTe", t - 2)))
                    for dc in range(NDC):
                        ins = te.transpose(
                            ps_bf[:, 2048 + q * 1024 + dc * 128:
                                  2048 + q * 1024 + dc * 128 + 128],
                            ptile[:, q * D + dc * 128: q * D + dc * 128 + 128],
                            sb_id[:, :])
                    E[("ptr", t)] = mark(ins, "pe")

            def emit_pe_transp(t):
                emit_pe_transp_xn(t)
                emit_pe_transp_p(t)

            emit_dve_pre(0)
            emit_dve_pre(1)
            for k in range(NT // 2):
                t0, t1 = 2 * k, 2 * k + 1

                @block.scalar
                def _(a, k=k, t0=t0, t1=t1):
                    wt(a, E[("xncast", t0)])
                    wt(a, E[("xncast", t1)])
                    wt(a, E.get(("pcast", t0 - 4)))
                    wt(a, E.get(("pcast", t1 - 4)))
                    if k >= 1:
                        wt(a, E[("xnTr", t0 - 2)])
                        wt(a, E[("xnTr", t0 - 1)])
                    if k < 6:
                        for t in (t0, t1):
                            tq = t % 4
                            xn32 = scr32[:, tq * D:(tq + 1) * D]
                            E[("exp", t)] = mark(
                                a.activation(xn32, xn32, AF.Exp,
                                             accum_out=st_s[:, t:t + 1]),
                                "ac")
                    if k >= 1:
                        emit_xnTe(a, t0 - 2)
                        emit_xnTe(a, t0 - 1)

                if k < NT // 2 - 1:
                    if k < 4:
                        emit_pass1(8 + 2 * k)
                        emit_pass1(9 + 2 * k)
                        if k == 1:
                            emit_pass2(4)
                            emit_pass2(5)
                        if k == 3:
                            emit_pass2(6)
                            emit_pass2(7)
                    emit_dve_pre(t0 + 2)
                    emit_dve_pre(t1 + 2)
                if k < 6:
                    emit_dve_post(t0)
                    emit_pe_transp(t0)
                    emit_dve_post(t1)
                    emit_pe_transp(t1)
                else:
                    emit_pe_transp_xn(t0)
                    emit_pe_transp_xn(t1)

            @block.scalar
            def _(a):
                wt(a, E[("xnTr", NT - 2)])
                wt(a, E[("xnTr", NT - 1)])
                emit_xnTe(a, NT - 2)
                emit_xnTe(a, NT - 1)


        with nc.psum_tensor("ps03", [128, 2048], F32) as ps03:

            def emit_m2(sg):
                for isub in range(4):
                    ii = sg * 4 + isub
                    q2 = isub % 2

                    @block.tensor
                    def _(te, sg=sg, isub=isub, ii=ii, q2=q2):
                        attA = ps03[:, q2 * 1024: q2 * 1024 + 512]
                        attB = ps03[:, q2 * 1024 + 512: q2 * 1024 + 1024]
                        den = ps47[:, q2 * 512: q2 * 512 + 1]
                        wt(te, E.get(("attev", ii - 2)))
                        wt(te, E[("scpe", sg, 6)])
                        if sg == 0 and isub < 2:
                            wt(te, E[("xnTe", NT - 1)])
                            wt(te, E["consts"])
                            if isub == 0:
                                wt(te, E[("pTe", 13)])
                            else:
                                wt(te, E[("pTe", NT - 1)])
                        for jc in range(NT):
                            if jc == 14:
                                wt(te, E[("scpe", sg, 7)])
                                if sg == 0 and isub == 0:
                                    wt(te, E[("pTe", NT - 1)])
                            lhsT = sc_sl(jc, isub * 128, 128)
                            st_ = (jc == 0)
                            sp_ = (jc == NT - 1)
                            te.matmul(attA, lhsT, xntok_sl(jc, 0, 512),
                                      start=st_, stop=sp_)
                            te.matmul(attB, lhsT, xntok_sl(jc, 512, 512),
                                      start=st_, stop=sp_)
                            ins = te.matmul(den, lhsT,
                                            rs_bf[:, jc:jc + 1],
                                            start=st_, stop=sp_)
                        E[("att", ii)] = mark(ins, "pe")

                    @block.vector
                    def _(v, ii=ii, q2=q2):
                        wt(v, E[("att", ii)])
                        den = ps47[:, q2 * 512: q2 * 512 + 1]
                        E[("recq", ii)] = e5 = mark(
                            v.reciprocal(recd[:, q2:q2 + 1], den), "dv")
                        wt(v, e5)
                        E[("attev", ii)] = mark(v.scalar_tensor_tensor(
                            out=xs(ii), in0=ps03[:, q2 * 1024:
                                                 (q2 + 1) * 1024],
                            scalar=recd[:, q2:q2 + 1], in1=xs(ii),
                            op0=ALU.mult, op1=ALU.add), "dv")
                E[("M2last", sg)] = E[("att", sg * 4 + 3)]

            @block.scalar
            def _(a):
                for t in range(12, 16):
                    wt(a, E[("xncast", t)])
                    wt(a, E[("pcast", t - 4)])
                for t in range(12, 16):
                    tq = t % 4
                    xn32 = scr32[:, tq * D:(tq + 1) * D]
                    E[("exp", t)] = mark(
                        a.activation(xn32, xn32, AF.Exp,
                                     accum_out=st_s[:, t:t + 1]), "ac")

            emit_dve_post(12, with_pte=True)
            emit_dve_post(13, with_pte=False)

            for p in range(4):
                emit_m1_pair_te(0, p)
                if p % 2 == 1:
                    emit_m1_evac(0, [p - 1, p])

            emit_pe_transp_p(12)
            emit_pe_transp_p(13)

            @block.vector
            def _(v):
                for t in (12, 13):
                    qp = t % 2
                    wt(v, E[("ptr", t)])
                    dstp = arena[:, PT0:PT0 + NDC * T].rearrange(
                        "p (dc t) -> p dc t", dc=NDC)[
                        :, :, t * 128:(t + 1) * 128]
                    srcp = ps_bf[:, 2048 + qp * 1024: 2048 + (qp + 1) * 1024
                                 ].rearrange("p (dc t) -> p dc t", dc=NDC)
                    E[("pTe", t)] = mark(
                        v.tensor_copy(out=dstp, in_=srcp), "dv")

            emit_dve_post(14, with_pte=False)
            emit_dve_post(15, with_pte=False)

            for p in range(4, 8):
                emit_m1_pair_te(0, p)
                if p % 2 == 1:
                    emit_m1_evac(0, [p - 1, p])
            E[("M1last", 0)] = E[("scp", 0, 7)]

            emit_pe_transp_p(14)
            emit_pe_transp_p(15)

            @block.vector
            def _(v):
                for t in (14, 15):
                    qp = t % 2
                    wt(v, E[("ptr", t)])
                    dstp = arena[:, PT0:PT0 + NDC * T].rearrange(
                        "p (dc t) -> p dc t", dc=NDC)[
                        :, :, t * 128:(t + 1) * 128]
                    srcp = ps_bf[:, 2048 + qp * 1024: 2048 + (qp + 1) * 1024
                                 ].rearrange("p (dc t) -> p dc t", dc=NDC)
                    E[("pTe", t)] = mark(
                        v.tensor_copy(out=dstp, in_=srcp), "dv")

            emit_m2(0)
            for sg in range(1, 4):
                for p in range(8):
                    emit_m1_pair_te(sg, p)
                    if p % 2 == 1:
                        emit_m1_evac(sg, [p - 1, p], first=(p == 1))
                E[("M1last", sg)] = E[("scp", sg, 7)]
                emit_m2(sg)

            grp_readers = {0: [E[("attev", 14)], E[("attev", 15)]],
                           1: [E[("scpe", 3, 7)], E[("attev", 15)]]}

            @block.sync
            def _(sy):
                sy.wait_ge(sems["ca"], cnt["ca"])
                for b in (0, 1):
                    b2 = b % 2
                    h0 = b * 512
                    sg_ = "g%d" % b2
                    sd_ = "d%d" % b2
                    wt(sy, E[("M1last", 3)])
                    for dc in range(NDC):
                        mark(sy.dma_start(
                            out=wgT_sl(b2, dc * 512, 512),
                            in_=wg_bf[h0:h0 + 512, dc * 128:(dc + 1) * 128],
                            transpose=True), sg_, 16)
                        mark(sy.dma_start(
                            out=wuT_sl(b2, dc * 512, 512),
                            in_=wu_bf[h0:h0 + 512, dc * 128:(dc + 1) * 128],
                            transpose=True), sg_, 16)
                    E[("Wgu", b)] = now(sg_)
                    wt(sy, E[("M2last", 3)])
                    for hcb in range(HBLK):
                        mark(sy.dma_start(
                            out=wdT_sl(b2, hcb * 1024, 1024),
                            in_=wd_bf[:, h0 + hcb * 128: h0 + (hcb + 1) * 128],
                            transpose=True), sd_, 16)
                    E[("Wd", b)] = now(sd_)

            for b in range(NBLK):
                b2 = b % 2
                h0 = b * 512
                last_g0_pass = None

                for hcb in range(HBLK):
                    hc = HBLK * b + hcb
                    for th in range(2):
                        pi = (hc * 2 + th + b) % 2
                        base = ps03 if pi == 0 else ps47
                        idx = hc * 2 + th
                        if pi == 0:
                            last_g0_pass = idx

                        @block.tensor
                        def _(te, b=b, b2=b2, hcb=hcb, th=th, pi=pi,
                              base=base, idx=idx):
                            if hcb == 0 and th == 0:
                                wt(te, E[("Wgu", b)])
                            rd = grp_readers[pi]
                            wt(te, rd[0])
                            for dc in range(NDC):
                                lg = wgT_sl(b2, dc * 512 + hcb * 128, 128)
                                st_ = (dc == 0)
                                sp_ = (dc == NDC - 1)
                                te.matmul(base[:, 0:512], lg,
                                          xnT_sl(dc, th * 1024, 512),
                                          start=st_, stop=sp_)
                                ig = te.matmul(base[:, 512:1024], lg,
                                               xnT_sl(dc, th * 1024 + 512,
                                                      512),
                                               start=st_, stop=sp_)
                            E[("m3g", idx)] = mark(ig, "pe")
                            for ev in rd[1:]:
                                wt(te, ev)
                            for dc in range(NDC):
                                lu = wuT_sl(b2, dc * 512 + hcb * 128, 128)
                                st_ = (dc == 0)
                                sp_ = (dc == NDC - 1)
                                te.matmul(base[:, 1024:1536], lu,
                                          xnT_sl(dc, th * 1024, 512),
                                          start=st_, stop=sp_)
                                ins = te.matmul(base[:, 1536:2048], lu,
                                                xnT_sl(dc, th * 1024 + 512,
                                                       512),
                                                start=st_, stop=sp_)
                            E[("m3", idx)] = mark(ins, "pe")

                    @block.scalar
                    def _(a, b=b, hc=hc):
                        for th in range(2):
                            idx = hc * 2 + th
                            pi = (idx + b) % 2
                            base = ps03 if pi == 0 else ps47
                            wt(a, E[("m3g", idx)])
                            wt(a, E.get(("hact", idx - 2)))
                            E[("silu", idx)] = mark(a.activation(
                                scr32[:, th * 1024:(th + 1) * 1024],
                                base[:, 0:1024], AF.Silu, scale=S127), "ac")

                    for th in range(2):
                        pi = (hc * 2 + th + b) % 2
                        base = ps03 if pi == 0 else ps47
                        idx = hc * 2 + th

                        @block.vector
                        def _(v, b=b, hcb=hcb, th=th, idx=idx, base=base):
                            wt(v, E[("silu", idx)])
                            wt(v, E[("m3", idx)])
                            if b == 0 and hcb == 0 and th == 0:
                                wt(v, E[("attev", 15)])
                            E[("hact", idx)] = mark(v.tensor_tensor(
                                out=hact_sl(hcb * 2048 + th * 1024, 1024),
                                in0=scr32[:, th * 1024:(th + 1) * 1024],
                                in1=base[:, 1024:2048],
                                op=ALU.mult), "dv")

                        grp_readers[pi] = [E[("silu", idx)],
                                           E[("hact", idx)]]

                E[("M3done", b)] = E[("m3", (HBLK * b + HBLK - 1) * 2 + 1)]

                for tt in range(NT):
                    q5 = tt % 2

                    @block.tensor
                    def _(te, b=b, b2=b2, tt=tt, q5=q5,
                          last_g0_pass=last_g0_pass):
                        if tt == 0:
                            wt(te, E[("Wd", b)])
                            wt(te, E[("silu", last_g0_pass)])
                        if tt == 1:
                            wt(te, E[("hact", last_g0_pass)])
                        wt(te, E.get(("sttev", b, tt - 2)))
                        for hcb in range(HBLK):
                            if tt == 0:
                                wt(te, E[("hact", (HBLK * b + hcb) * 2 + 1)])
                            lh = hact_sl(hcb * 2048 + tt * 128, 128)
                            st_ = (hcb == 0)
                            sp_ = (hcb == HBLK - 1)
                            te.matmul(ps03[:, q5 * 1024: q5 * 1024 + 512],
                                      lh, wdT_sl(b2, hcb * 1024, 512),
                                      start=st_, stop=sp_)
                            ins = te.matmul(
                                ps03[:, q5 * 1024 + 512: q5 * 1024 + 1024],
                                lh, wdT_sl(b2, hcb * 1024 + 512, 512),
                                start=st_, stop=sp_)
                        E[("m5", b, tt)] = mark(ins, "pe")

                    @block.vector
                    def _(v, b=b, tt=tt, q5=q5):
                        wt(v, E[("m5", b, tt)])
                        E[("sttev", b, tt)] = mark(v.scalar_tensor_tensor(
                            out=xs(tt),
                            in0=ps03[:, q5 * 1024:(q5 + 1) * 1024],
                            scalar=S127SQ, in1=xs(tt),
                            op0=ALU.mult, op1=ALU.add), "dv")
                E[("M5done", b)] = E[("m5", b, NT - 1)]
                grp_readers[0] = [E[("sttev", b, NT - 2)],
                                  E[("sttev", b, NT - 1)]]

                if b + 2 < NBLK:
                    @block.sync
                    def _(sy, b=b):
                        bn = b + 2
                        b2n = bn % 2
                        h0n = bn * 512
                        sg_ = "g%d" % b2n
                        sd_ = "d%d" % b2n
                        wt(sy, E[("M3done", b)])
                        for dc in range(NDC):
                            mark(sy.dma_start(
                                out=wgT_sl(b2n, dc * 512, 512),
                                in_=wg_bf[h0n:h0n + 512,
                                          dc * 128:(dc + 1) * 128],
                                transpose=True), sg_, 16)
                            mark(sy.dma_start(
                                out=wuT_sl(b2n, dc * 512, 512),
                                in_=wu_bf[h0n:h0n + 512,
                                          dc * 128:(dc + 1) * 128],
                                transpose=True), sg_, 16)
                        E[("Wgu", bn)] = now(sg_)
                        wt(sy, E[("M5done", b)])
                        for hcb in range(HBLK):
                            mark(sy.dma_start(
                                out=wdT_sl(b2n, hcb * 1024, 1024),
                                in_=wd_bf[:, h0n + hcb * 128:
                                          h0n + (hcb + 1) * 128],
                                transpose=True), sd_, 16)
                        E[("Wd", bn)] = now(sd_)

            @block.sync
            def _(sy):
                for tt in range(NT):
                    wt(sy, E[("sttev", NBLK - 1, tt)])
                    mark(sy.dma_start(out=o_r[:, tt, :], in_=xs(tt)),
                         "st", 16)
                sy.wait_ge(sems["st"], cnt["st"])

    return nc


_IDENT = np.eye(128, dtype=ml_dtypes.bfloat16)


def kernel(L0_attn_x, L0_ffn_w_gate, L0_ffn_w_up, L0_ffn_w_down):
    x = np.ascontiguousarray(np.asarray(L0_attn_x, dtype=np.float32))
    wg = np.ascontiguousarray(np.asarray(L0_ffn_w_gate, dtype=np.int32))
    wu = np.ascontiguousarray(np.asarray(L0_ffn_w_up, dtype=np.int32))
    wd = np.ascontiguousarray(np.asarray(L0_ffn_w_down, dtype=np.int32))

    nc = build_nc()
    in_maps = [{"x": x[c], "wg": wg, "wu": wu, "wd": wd, "ident": _IDENT}
               for c in range(8)]
    res = run_bass_kernel_spmd(nc, in_maps, core_ids=list(range(8)))
    return np.stack([np.asarray(r["o"]) for r in res.results], axis=0)
